# version 04
# baseline (speedup 1.0000x reference)
import numpy as np
from contextlib import ExitStack
import concourse.bass as bass
import concourse.mybir as mybir
from concourse.bass_utils import run_bass_kernel_spmd

F32 = mybir.dt.float32
BF16 = mybir.dt.bfloat16
U8 = mybir.dt.uint8
AF = mybir.ActivationFunctionType
ALU = mybir.AluOpType
AX = mybir.AxisListType

B, T, D, H, DH, C, NCH, FF, E = 8, 4096, 1024, 16, 64, 128, 32, 4096, 8
CAP = 8192
EPS = 1e-6
LNEPS = 1e-5
NQ = 16
Q_SDK, Q_SDQ, Q_W3, Q_W4, Q_E, Q_QCK, Q_QCKS, Q_KCQ, Q_KCQS, Q_SINK, Q_SRC, \
    Q_D1, Q_D2, Q_CS, Q_T1, Q_T2 = range(16)

DEBUG_TAPS = []
PHASE_LIMIT = None
P2B_UPTO = 99
LAST_TAPS = {}

ENGS = ("sync", "tensor", "vector", "scalar", "gpsimd")


SEMS = ("sync_d", "gp_d", "sc_d", "tensor", "vector", "scalar", "gpsimd", "cc")


def _sem_of(eng, dma):
    if dma:
        return {"sync": "sync_d", "gpsimd": "gp_d", "scalar": "sc_d"}[eng]
    return eng


class Ctx:

    def __init__(self, nc):
        self.nc = nc
        self.prog = []
        self.cnt = {s: 0 for s in SEMS}
        self.lastw = {}
        self.readers = {}
        self.waited = {e: {s: 0 for s in SEMS} for e in ENGS}

    def _need(self, eng, dma, keys_r, keys_w):
        skip = eng if (eng == "tensor" and not dma) else None
        need = {}

        def add(s, v):
            if s == skip:
                return
            need[s] = max(need.get(s, 0), v)

        for k in keys_r:
            wv = self.lastw.get(k)
            if wv:
                add(wv[0], wv[1])
        for k in keys_w:
            wv = self.lastw.get(k)
            if wv:
                add(wv[0], wv[1])
            for rs, rc in self.readers.get(k, {}).items():
                add(rs, rc)
        waits = []
        for s, v in need.items():
            if self.waited[eng][s] < v:
                waits.append((s, v))
                self.waited[eng][s] = v
        return waits

    def _update(self, sem, keys_r, keys_w):
        c = self.cnt[sem]
        for k in keys_r:
            self.readers.setdefault(k, {})[sem] = c
        for k in keys_w:
            self.lastw[k] = (sem, c)
            self.readers[k] = {}

    @staticmethod
    def _psum_promote(r, w):
        rr = [k for k in r if not k.startswith("pb")]
        ww = list(w) + [k for k in r if k.startswith("pb")]
        return rr, ww

    def op(self, eng, emit, r=(), w=(), dma=False):
        import sys as _sys
        ln = _sys._getframe(1).f_lineno
        r, w = self._psum_promote(r, w)
        sem = _sem_of(eng, dma)
        waits = self._need(eng, dma, r, w)
        inc = 16 if dma else 1
        self.cnt[sem] += inc
        self.prog.append((eng, waits, emit, (sem, inc), ln))
        self._update(sem, r, w)

    def group(self, eng, emits, r=(), w=()):
        import sys as _sys
        ln = _sys._getframe(1).f_lineno
        r, w = self._psum_promote(r, w)
        sem = _sem_of(eng, False)
        waits = self._need(eng, False, r, w)
        self.cnt[sem] += 1
        self.prog.append((eng, waits, emits, ("group", sem, 1), ln))
        self._update(sem, r, w)

    def barrier(self):
        for e in ENGS:
            for s in SEMS:
                if s == e:
                    continue
                if self.waited[e][s] < self.cnt[s]:
                    self.prog.append((e, [(s, self.cnt[s])], None, (None, 0), 0))
                    self.waited[e][s] = self.cnt[s]

    def finalize(self, block, sems):
        def body(engine_obj, eng_name):
            for eng, waits, emit, inc, _ln in self.prog:
                if eng != eng_name:
                    continue
                for s, v in waits:
                    engine_obj.wait_ge(sems[s], v)
                if emit is None:
                    continue
                if inc[0] == "group":
                    insts = emit(engine_obj)
                    insts[-1].then_inc(sems[inc[1]], inc[2])
                else:
                    inst = emit(engine_obj)
                    inst.then_inc(sems[inc[0]], inc[1])
        block.sync(lambda e: body(e, "sync"))
        block.tensor(lambda e: body(e, "tensor"))
        block.vector(lambda e: body(e, "vector"))
        block.scalar(lambda e: body(e, "scalar"))
        block.gpsimd(lambda e: body(e, "gpsimd"))


def stq(sST, q):
    return sST[:, :].rearrange("p (h q) -> p h q", q=NQ)[:, :, q]


def build_program():
    nc = bass.Bass("TRN2", target_bir_lowering=False, detect_race_conditions=False)

    X_in = nc.dram_tensor("x", [T, D], F32, kind="ExternalInput").ap()
    cid = nc.dram_tensor("cid", [1, 8], F32, kind="ExternalInput").ap()
    wnames = {
        "Wq": [D, D], "bq": [1, D], "Wk": [D, D], "bk": [1, D],
        "Wv": [D, D], "bv": [1, D], "Wo": [D, D], "bo": [1, D],
        "lg1": [1, D], "lb1": [1, D], "W1": [D, FF], "b1": [1, FF],
        "W2": [FF, D], "b2": [1, D], "lg2": [1, D], "lb2": [1, D],
        "fg1": [1, D], "fb1": [1, D], "Wqkv": [D, 3 * D], "Wproj": [D, D],
        "fg2": [1, D], "fb2": [1, D], "Wg": [D, E],
        "eW1": [E * D, FF], "eb1": [E, FF], "eW2": [E * FF, D], "eb2": [E, D],
    }
    W = {k: nc.dram_tensor(k, sh, F32, kind="ExternalInput").ap()
         for k, sh in wnames.items()}
    OUT = nc.dram_tensor("out", [T, D], F32, kind="ExternalOutput").ap()

    taps = {}
    for tn in DEBUG_TAPS:
        shp = {
            "qt": [D, T], "kt": [D, T], "qtok": [T, D], "ktok": [T, D],
            "v": [T, D], "xattn": [T, D], "h": [T, D], "s2": [T, D],
            "x1": [T, D], "ktrope": [D, T], "v65": [T, D], "qtrope": [D, T],
            "attno": [T, D], "x2": [T, D], "h2": [T, D], "gates": [T, E],
            "routing": [T, 3 * E], "negm": [H, T], "moe": [T, D],
            "la0": [128, H * NQ], "la1": [128, H * NQ],
            "rhsq": [128, 48], "qcols": [128, 96],
            "sab0": [128, 256], "md0": [128, 160],
            "d1a": [128, 16], "d1b": [128, 16],
            "oraw0": [128, H * 65], "kv1": [128, 8 * DH],
            "crow1": [1, 5 * H], "carry1": [64, 4 * H],
        }[tn]
        taps[tn] = nc.dram_tensor("tap_" + tn, shp, F32, kind="ExternalOutput").ap()

    h_sp = nc.dram_tensor("h_sp", [T, D], F32).ap()
    s2_sp = nc.dram_tensor("s2_sp", [T, D], F32).ap()
    x1_sp = nc.dram_tensor("x1_sp", [T, D], F32).ap()
    qt_sp = nc.dram_tensor("qt_sp", [NCH, 128, D], BF16).ap()
    kt_sp = nc.dram_tensor("kt_sp", [NCH, 128, D], BF16).ap()
    ao_sp = nc.dram_tensor("ao_sp", [T, D], BF16).ap()
    w_sp = nc.dram_tensor("w_sp", [T, D], F32).ap()
    h2t_sp = nc.dram_tensor("h2t_sp", [8, 128, T], BF16).ap()
    acc_sp = nc.dram_tensor("acc_sp", [T, D], F32).ap()
    cc_in = nc.dram_tensor("cc_in", [1, E], F32).ap()
    cc_out = nc.dram_tensor("cc_out", [B, E], F32, addr_space="Shared").ap()

    es = ExitStack()
    stacks = {"cur": es}
    def sb(name, shape, dt=F32):
        return stacks["cur"].enter_context(nc.sbuf_tensor(name, shape, dt))

    TRIUf = sb("TRIUf", [128, 128])
    TRIUb = sb("TRIUb", [128, 128], BF16)
    SLOB = sb("SLOB", [128, 128])
    idb = sb("idb", [128, 128], BF16)
    sG = sb("sG", [128, NCH])
    sRG = sb("sRG", [128, NCH])
    onesCol = sb("onesCol", [128, 2])
    onesColB = sb("onesColB", [128, 2], BF16)
    zRow = sb("zRow", [1, 512])
    oneRow = sb("oneRow", [1, 512])
    sIt = sb("sIt", [128, 128])

    sems = {}
    pb = [es.enter_context(nc.psum_tensor(f"pb{k}", [128, 512], F32)) for k in range(8)]
    PBK = [f"pb{k}" for k in range(8)]

    def pbank(k, c0, w):
        return pb[k][:, c0:c0 + w]

    def pbankb(k, c0, w):
        return pb[k].ap().bitcast(BF16)[:, c0:c0 + w]

    for s in SEMS:
        sems[s] = es.enter_context(nc.semaphore("sem_" + s))

    block = es.enter_context(nc.Block())
    cx = Ctx(nc)

    cx.op("gpsimd", lambda e: e.iota(sIt[:, :], pattern=[[-1, 128]], base=0,
                                     channel_multiplier=1,
                                     allow_small_or_imprecise_dtypes=True),
          w=["sIt"])
    cx.op("vector", lambda e: e.tensor_single_scalar(TRIUf[:, :], sIt[:, :], 0.0,
                                                     op=ALU.is_le),
          r=["sIt"], w=["TRIUf"])
    cx.op("vector", lambda e: e.tensor_copy(TRIUb[:, :], TRIUf[:, :]),
          r=["TRIUf"], w=["TRIUb"])
    cx.op("vector", lambda e: e.tensor_single_scalar(SLOB[:, :], sIt[:, :], 0.0,
                                                     op=ALU.is_gt),
          r=["sIt"], w=["SLOB"])
    cx.op("vector", lambda e: e.tensor_scalar_mul(SLOB[:, :], SLOB[:, :], -1e9),
          r=["SLOB"], w=["SLOB"])
    cx.op("vector", lambda e: e.tensor_single_scalar(idb[:, :], sIt[:, :], 0.0,
                                                     op=ALU.is_equal),
          r=["sIt"], w=["idb"])
    cx.op("gpsimd", lambda e: e.iota(sG[:, :], pattern=[[128, NCH]], base=1,
                                     channel_multiplier=1,
                                     allow_small_or_imprecise_dtypes=True),
          w=["sG"])
    cx.op("vector", lambda e: e.reciprocal(sRG[:, :], sG[:, :]), r=["sG"], w=["sRG"])
    cx.op("vector", lambda e: e.memset(onesCol[:, :], 1.0), w=["onesCol"])
    cx.op("vector", lambda e: e.memset(onesColB[:, :], 1.0), w=["onesColB"])
    cx.op("vector", lambda e: e.memset(zRow[:, :], 0.0), w=["zRow"])
    cx.op("vector", lambda e: e.memset(oneRow[:, :], 1.0), w=["oneRow"])

    sHb = sb("sHb", [128, D], BF16)
    sX = [sb(f"sX{i}", [128, D]) for i in range(2)]
    sXT = sb("sXT", [128, D], BF16)
    sR = sb("sR", [128, D])
    sHt = sb("sHt", [128, D])
    sSq = sb("sSq", [128, D], BF16)
    sCol = sb("sCol", [128, 4])
    sFin = sb("sFin", [128, D])
    sPGall = sb("sPGall", [128, NCH * 8])

    es_p1 = ExitStack()
    stacks["cur"] = es_p1
    sXb = [sb(f"sXb{i}", [128, D], BF16) for i in range(2)]
    def load_folded(name, src, rows, cols):
        t = sb(name, [128, (rows // 128) * cols], BF16)
        cx.op("gpsimd",
              lambda e, t=t, src=src, rows=rows, cols=cols: e.dma_start(
                  out=t[:, :].rearrange("p (g n) -> p g n", g=rows // 128),
                  in_=src.rearrange("(g p) n -> p g n", p=128)),
              w=[name], dma=True)
        return t

    sWq = load_folded("sWq", W["Wq"], D, D)
    sWk = load_folded("sWk", W["Wk"], D, D)
    sWv = load_folded("sWv", W["Wv"], D, D)
    sWo = load_folded("sWo", W["Wo"], D, D)

    sBq2 = sb("sBq2", [128, 8])
    sBk2 = sb("sBk2", [128, 8])
    for g in range(8):
        cx.op("sync", lambda e, g=g: e.dma_start(
            out=sBq2[:, g:g + 1], in_=W["bq"][0:1, g * 128:(g + 1) * 128].rearrange("o p -> p o")),
            w=["sBq2"], dma=True)
        cx.op("sync", lambda e, g=g: e.dma_start(
            out=sBk2[:, g:g + 1], in_=W["bk"][0:1, g * 128:(g + 1) * 128].rearrange("o p -> p o")),
            w=["sBk2"], dma=True)
    cx.op("vector", lambda e: e.tensor_scalar_mul(sBq2[:, :], sBq2[:, :], 0.5),
          r=["sBq2"], w=["sBq2"])
    cx.op("vector", lambda e: e.tensor_scalar_mul(sBk2[:, :], sBk2[:, :], 0.5),
          r=["sBk2"], w=["sBk2"])
    sBv = sb("sBv", [1, D])
    sBo = sb("sBo", [1, D])
    sBqr = sb("sBqr", [1, D])
    sBkr = sb("sBkr", [1, D])
    for nm, dst in (("bv", sBv), ("bo", sBo), ("bq", sBqr), ("bk", sBkr)):
        cx.op("sync", lambda e, nm=nm, dst=dst: e.dma_start(out=dst[:, :], in_=W[nm]),
              w=[nm + "_r"], dma=True)
    sLg1r = sb("sLg1r", [1, D])
    sLb1r = sb("sLb1r", [1, D])
    cx.op("sync", lambda e: e.dma_start(out=sLg1r[:, :], in_=W["lg1"]), w=["lg1r"], dma=True)
    cx.op("sync", lambda e: e.dma_start(out=sLb1r[:, :], in_=W["lb1"]), w=["lb1r"], dma=True)
    sLg1 = sb("sLg1", [128, D])
    sLb1 = sb("sLb1", [128, D])

    def replicate_row(row_t, out_t, rkey, wkey, bank=0):
        for half in range(2):
            cx.group("tensor", lambda e, half=half: [
                e.matmul(pbank(bank, 0, 512), oneRow[:1, 0:128],
                         row_t[:, half * 512:(half + 1) * 512], start=True, stop=True)],
                r=[rkey, "onesCol"], w=[PBK[bank]])
            cx.op("vector", lambda e, half=half: e.tensor_copy(
                out_t[:, half * 512:(half + 1) * 512], pbank(bank, 0, 512)),
                r=[PBK[bank]], w=[wkey])

    replicate_row(sLg1r, sLg1, "lg1r", "sLg1")
    replicate_row(sLb1r, sLb1, "lb1r", "sLb1")

    sQT = sb("sQT", [128, D], BF16)
    sKT = sb("sKT", [128, D], BF16)
    sQtok = sb("sQtok", [128, D], BF16)
    sKtok = sb("sKtok", [128, D], BF16)
    sV = sb("sV", [128, D], BF16)
    sAB = sb("sAB", [128, 2 * H * 128], BF16)
    sST = sb("sST", [128, H * NQ])
    sVs = sb("sVs", [128, H * 65], BF16)
    sOS = sb("sOS", [128, 2 * H], BF16)
    sOS2 = sb("sOS2", [128, 2 * H], BF16)
    sOutRaw = sb("sOutRaw", [128, H * 65])
    sXattn = sb("sXattn", [128, D], BF16)
    sXaT = sb("sXaT", [128, D], BF16)
    sRHSq = sb("sRHSq", [128, 48], BF16)
    sRHSk = sb("sRHSk", [128, 48], BF16)
    sKV2 = sb("sKV2", [128, 8 * DH], BF16)
    sCarry = sb("sCarry", [64, 4 * H])
    sCRow = sb("sCRow", [1, 5 * H])
    sCR2 = sb("sCR2", [1, 2 * H])
    sCR3 = sb("sCR3", [1, 2 * H])
    sCR4 = sb("sCR4", [1, H])
    sGam = sb("sGam", [128, H])
    sSinkB = sb("sSinkB", [128, H], BF16)
    sTmpA = sb("sTmpA", [128, H])
    sTmpB = sb("sTmpB", [128, H])

    cx.op("vector", lambda e: e.memset(sST[:, :], 0.0), w=["sST"])
    cx.op("vector", lambda e: e.memset(sCarry[:, :], 0.0), w=["sCarry"])
    cx.op("vector", lambda e: e.memset(sCRow[:, :], 0.0), w=["sCRow"])
    cx.op("vector", lambda e: e.memset(sRHSq[:, :], 0.0), w=["sRHSq"])
    cx.op("vector", lambda e: e.memset(sRHSk[:, :], 0.0), w=["sRHSk"])
    cx.op("vector", lambda e: e.memset(sOS[:, :], 0.0), w=["sOS"])
    cx.op("vector", lambda e: e.memset(sOS2[:, :], 0.0), w=["sOS2"])
    def rhs_ones(t, key):
        cx.op("vector", lambda e, t=t: e.memset(
            t[0:64, :].rearrange("p (g c) -> p g c", c=6)[:, :, 2], 1.0),
            r=[key], w=[key])
        cx.op("vector", lambda e, t=t: e.memset(
            t[64:128, :].rearrange("p (g c) -> p g c", c=6)[:, :, 5], 1.0),
            r=[key], w=[key])
    rhs_ones(sRHSq, "sRHSq")
    rhs_ones(sRHSk, "sRHSk")
    for t, key in ((sOS, "sOS"), (sOS2, "sOS2")):
        cx.op("vector", lambda e, t=t: e.memset(
            t[:, :].rearrange("p (h c) -> p h c", c=2)[:, :, 0], 1.0),
            r=[key], w=[key])

    for k in (6, 7):
        cx.group("tensor", lambda e, k=k: [
            e.matmul(pbank(k, 0, 512)[0:64, :], oneRow[:1, 0:64],
                     zRow[:, 0:512], start=True, stop=True)],
            r=["onesCol", "zRow"], w=[PBK[k]])

    rot = [0]

    def mm128(lhsT_fn, rhs_fn, rkeys, evac, n_acc=8, extra=None, N=128):
        bk = rot[0]
        rot[0] = 1 - rot[0]
        def emits(e, bk=bk):
            insts = []
            for kb in range(n_acc):
                insts.append(e.matmul(pbank(bk, 0, N), lhsT_fn(kb), rhs_fn(kb),
                                      start=(kb == 0),
                                      stop=(extra is None and kb == n_acc - 1)))
            if extra is not None:
                insts.append(extra(e, pbank(bk, 0, N)))
            return insts
        cx.group("tensor", emits, r=list(rkeys) + ["onesCol"], w=[PBK[bk]])
        evac(bk)

    def emit_p1_chunk(c):
        t0 = c * C
        pX = sX[c % 2]
        pXb = sXb[c % 2]
        xk = f"sX{c % 2}"
        xbk = f"sXb{c % 2}"
        cx.op("sync", lambda e: e.dma_start(out=pX[:, :], in_=X_in[t0:t0 + C, :]),
              w=[xk], dma=True)
        cx.op("vector", lambda e: e.tensor_copy(pXb[:, :], pX[:, :]),
              r=[xk], w=[xbk])
        for g in range(8):
            bk = rot[0]; rot[0] = 1 - rot[0]
            cx.group("tensor", lambda e, g=g, bk=bk: [
                e.transpose(pbankb(bk, 0, 128), pXb[:, g * 128:(g + 1) * 128], idb[:, :])],
                r=[xbk, "idb"], w=[PBK[bk]])
            cx.op("scalar", lambda e, g=g, bk=bk: e.copy(
                sXT[:, g * 128:(g + 1) * 128], pbankb(bk, 0, 128)),
                r=[PBK[bk]], w=["sXT"])

        for (sWt, wkey, outT, okey, bias2) in (
                (sWq, "sWq", sQT, "sQT", sBq2), (sWk, "sWk", sKT, "sKT", sBk2)):
            for m in range(8):
                def evac(bk, m=m, outT=outT, okey=okey, bias2=bias2):
                    cx.op("scalar", lambda e: e.activation(
                        outT[:, m * 128:(m + 1) * 128], pbank(bk, 0, 128), AF.Tanh,
                        bias=bias2[:, m:m + 1], scale=0.5),
                        r=[PBK[bk], "sBq2" if bias2 is sBq2 else "sBk2"], w=[okey])
                    cx.op("gpsimd", lambda e: e.tensor_scalar(
                        outT[:, m * 128:(m + 1) * 128], outT[:, m * 128:(m + 1) * 128],
                        0.5, 0.5, op0=ALU.mult, op1=ALU.add),
                        r=[okey], w=[okey])
                mm128(lambda kb, sWt=sWt, m=m: sWt[:, kb * D + m * 128: kb * D + (m + 1) * 128],
                      lambda kb: sXT[:, kb * 128:(kb + 1) * 128],
                      [wkey, "sXT"], evac)
        for (sWt, wkey, outT, okey, brow, bkey, sigm) in (
                (sWq, "sWq", sQtok, "sQtok", sBqr, "bq_r", True),
                (sWk, "sWk", sKtok, "sKtok", sBkr, "bk_r", True),
                (sWv, "sWv", sV, "sV", sBv, "bv_r", False)):
            for m in range(8):
                def extra(e, ps, m=m, brow=brow):
                    return e.matmul(ps, oneRow[:1, 0:128],
                                    brow[:, m * 128:(m + 1) * 128], start=False, stop=True)
                def evac(bk, m=m, outT=outT, okey=okey, sigm=sigm):
                    if sigm:
                        cx.op("scalar", lambda e: e.activation(
                            outT[:, m * 128:(m + 1) * 128], pbank(bk, 0, 128),
                            AF.Tanh, scale=0.5), r=[PBK[bk]], w=[okey])
                        cx.op("gpsimd", lambda e: e.tensor_scalar(
                            outT[:, m * 128:(m + 1) * 128], outT[:, m * 128:(m + 1) * 128],
                            0.5, 0.5, op0=ALU.mult, op1=ALU.add),
                            r=[okey], w=[okey])
                    else:
                        cx.op("scalar", lambda e: e.copy(
                            outT[:, m * 128:(m + 1) * 128], pbank(bk, 0, 128)),
                            r=[PBK[bk]], w=[okey])
                mm128(lambda kb, m=m: sXT[:, kb * 128:(kb + 1) * 128],
                      lambda kb, sWt=sWt, m=m: sWt[:, kb * D + m * 128: kb * D + (m + 1) * 128],
                      [wkey, "sXT", bkey, "oneRow"], evac, extra=extra)

        if "qt" in taps or "kt" in taps or "qtok" in taps or "ktok" in taps or "v" in taps:
            emit_p1_taps(c)

        emit_la_chunk_math(c)
        emit_wo_ln1(c)

    def emit_p1_taps(c):
        t0 = c * C
        for tn, src in (("qt", sQT), ("kt", sKT), ("qtok", sQtok),
                        ("ktok", sKtok), ("v", sV)):
            if tn not in taps:
                continue
            cx.op("vector", lambda e, src=src: e.tensor_copy(sR[:, :], src[:, :]),
                  r=["sQT", "sKT", "sQtok", "sKtok", "sV"], w=["sR"])
            if tn in ("qt", "kt"):
                cx.op("sync", lambda e, tn=tn, t0=t0: e.dma_start(
                    out=taps[tn].rearrange("(g p) t -> p g t", p=128)[:, :, t0:t0 + C],
                    in_=sR[:, :].rearrange("p (g n) -> p g n", g=8)),
                    r=["sR"], w=["tap" + tn], dma=True)
            else:
                cx.op("sync", lambda e, tn=tn, t0=t0: e.dma_start(
                    out=taps[tn][t0:t0 + C, :], in_=sR[:, :]),
                    r=["sR"], w=["tap" + tn], dma=True)

    def emit_la_chunk_math(c):
        for bk_i, h0 in ((6, 0), (7, 8)):
            cx.op("vector", lambda e, bk_i=bk_i, h0=h0: e.tensor_copy(
                sKV2[0:64, :].rearrange("p (g n) -> p g n", n=DH)[:, (h0 // 2):(h0 // 2) + 4, :],
                pbank(bk_i, 0, 512)[0:64, :].rearrange("p (g n) -> p g n", n=DH)[:, 0::2, :]),
                r=[PBK[bk_i]], w=["sKV2"])
            cx.op("vector", lambda e, bk_i=bk_i, h0=h0: e.tensor_copy(
                sKV2[64:128, :].rearrange("p (g n) -> p g n", n=DH)[:, (h0 // 2):(h0 // 2) + 4, :],
                pbank(bk_i, 0, 512)[0:64, :].rearrange("p (g n) -> p g n", n=DH)[:, 1::2, :]),
                r=[PBK[bk_i]], w=["sKV2"])

        car = lambda j, par: sCarry[:, :].rearrange("p (h j) -> p h j", j=4)[:, par::2, j]
        for par in range(2):
            rows = slice(par * 64, par * 64 + 64)
            q_cols = lambda off, rows=rows, par=par: sRHSq[rows, :].rearrange("p (g c) -> p g c", c=6)[:, :, off + 3 * par]
            k_cols = lambda off, rows=rows, par=par: sRHSk[rows, :].rearrange("p (g c) -> p g c", c=6)[:, :, off + 3 * par]
            cx.op("vector", lambda e, par=par, q_cols=q_cols: e.tensor_scalar(
                q_cols(0), car(0, par), EPS, None, op0=ALU.add),
                r=["sCarry"], w=["sRHSq"])
            cx.op("vector", lambda e, par=par, q_cols=q_cols: e.tensor_copy(
                q_cols(1), car(1, par)), r=["sCarry"], w=["sRHSq"])
            cx.op("vector", lambda e, par=par, k_cols=k_cols: e.tensor_scalar(
                k_cols(0), car(2, par), EPS, None, op0=ALU.add),
                r=["sCarry"], w=["sRHSk"])
            cx.op("vector", lambda e, par=par, k_cols=k_cols: e.tensor_copy(
                k_cols(1), car(3, par)), r=["sCarry"], w=["sRHSk"])

        crj = lambda j: sCRow[:, :].rearrange("o (h j) -> o h j", j=5)[:, :, j]
        cx.op("vector", lambda e: e.tensor_copy(sCR2[:, 0::2], crj(0)), r=["sCRow"], w=["sCR2"])
        cx.op("vector", lambda e: e.tensor_copy(sCR2[:, 1::2], crj(1)), r=["sCRow"], w=["sCR2"])
        cx.op("vector", lambda e: e.tensor_scalar(sCR2[:, :], sCR2[:, :], 64.0 * EPS, None,
                                                  op0=ALU.add), r=["sCR2"], w=["sCR2"])
        cx.op("vector", lambda e: e.tensor_copy(sCR3[:, 0::2], crj(2)), r=["sCRow"], w=["sCR3"])
        cx.op("vector", lambda e: e.tensor_copy(sCR3[:, 1::2], crj(3)), r=["sCRow"], w=["sCR3"])
        cx.op("vector", lambda e: e.tensor_scalar(sCR3[:, :], sCR3[:, :], 64.0 * EPS, None,
                                                  op0=ALU.add), r=["sCR3"], w=["sCR3"])
        cx.op("vector", lambda e: e.tensor_copy(sCR4[:, :], crj(4)), r=["sCRow"], w=["sCR4"])

        for g in range(8):
            cx.group("tensor", lambda e, g=g: [
                e.matmul(pbank(3, g * 6, 6), sQT[:, g * 128:(g + 1) * 128],
                         sRHSq[:, g * 6:(g + 1) * 6], start=(g == 0), stop=True, skip_group_check=True)],
                r=["sQT", "sRHSq"], w=[PBK[3]])
        for g in range(8):
            cx.group("tensor", lambda e, g=g: [
                e.matmul(pbank(3, 48 + g * 6, 6), sKT[:, g * 128:(g + 1) * 128],
                         sRHSk[:, g * 6:(g + 1) * 6], start=False, stop=True, skip_group_check=True)],
                r=["sKT", "sRHSk"], w=[PBK[3]])
        for par in range(2):
            base = 3 * par
            ps6 = lambda off, base=base: pbank(3, 0, 48).rearrange("p (g c) -> p g c", c=6)[:, :, base + off]
            ps6k = lambda off, base=base: pbank(3, 48, 48).rearrange("p (g c) -> p g c", c=6)[:, :, base + off]
            dst = lambda q, par=par: sST[:, :].rearrange("p (h q) -> p h q", q=NQ)[:, par::2, q]
            cx.op("vector", lambda e, f=ps6, d=dst: e.tensor_copy(d(Q_QCK), f(0)),
                  r=[PBK[3]], w=["sST"])
            cx.op("vector", lambda e, f=ps6, d=dst: e.tensor_copy(d(Q_QCKS), f(1)),
                  r=[PBK[3]], w=["sST"])
            cx.op("vector", lambda e, f=ps6, d=dst: e.tensor_copy(d(Q_SDQ), f(2)),
                  r=[PBK[3]], w=["sST"])
            cx.op("vector", lambda e, f=ps6k, d=dst: e.tensor_copy(d(Q_KCQ), f(0)),
                  r=[PBK[3]], w=["sST"])
            cx.op("vector", lambda e, f=ps6k, d=dst: e.tensor_copy(d(Q_KCQS), f(1)),
                  r=[PBK[3]], w=["sST"])
            cx.op("vector", lambda e, f=ps6k, d=dst: e.tensor_copy(d(Q_SDK), f(2)),
                  r=[PBK[3]], w=["sST"])

        if c == 0 and "rhsq" in taps:
            cx.op("vector", lambda e: e.tensor_copy(sR[:, 0:48], sRHSq[:, :]),
                  r=["sRHSq"], w=["sR"])
            cx.op("sync", lambda e: e.dma_start(out=taps["rhsq"], in_=sR[:, 0:48]),
                  r=["sR"], w=["taprq"], dma=True)
        if c == 0 and "qcols" in taps:
            cx.op("vector", lambda e: e.tensor_copy(sR[:, 0:96], pbank(3, 0, 96)),
                  r=[PBK[3]], w=["sR"])
            cx.op("sync", lambda e: e.dma_start(out=taps["qcols"], in_=sR[:, 0:96]),
                  r=["sR"], w=["tapqc"], dma=True)
        tri_rhs = sST[:, :].rearrange("p (h q) -> p h q", q=NQ)[:, :, Q_SDK:Q_SDQ + 1]
        cx.group("tensor", lambda e: [
            e.matmul(pbank(3, 96, 32), TRIUf[:, :], tri_rhs, start=False, stop=True, skip_group_check=True),
            e.matmul(pbank(3, 96, 32), oneRow[:1, 0:128],
                     sCR2[:, :], start=False, stop=True, skip_group_check=True)],
            r=["TRIUf", "sST", "sCR2", "oneRow"], w=[PBK[3]])

        for h in range(H):
            par, g = h % 2, h // 2
            rows = slice(par * 64, par * 64 + 64)
            qa = sQT[rows, g * 128:(g + 1) * 128]
            ka = sKT[rows, g * 128:(g + 1) * 128]
            slot = h % 4
            bkS = 2
            cx.group("tensor", lambda e, qa=qa, ka=ka, slot=slot: [
                e.matmul(pbank(bkS, slot * 128, 128), qa, ka, start=True, stop=True)],
                r=["sQT", "sKT"], w=[PBK[bkS]])
            cx.op("vector", lambda e, h=h, slot=slot: e.tensor_mul(
                sAB[:, (2 * h) * 128:(2 * h + 1) * 128], pbank(bkS, slot * 128, 128),
                TRIUf[:, :]), r=[PBK[bkS], "TRIUf"], w=["sAB"])
            slot2 = (h % 4) if True else 0
            cx.group("tensor", lambda e, qa=qa, ka=ka, slot2=slot2: [
                e.matmul(pbank(bkS, slot2 * 128, 128), ka, qa, start=True, stop=True)],
                r=["sQT", "sKT", "sAB"], w=[PBK[bkS]])
            cx.op("vector", lambda e, h=h, slot2=slot2: e.tensor_mul(
                sAB[:, (2 * h + 1) * 128:(2 * h + 2) * 128], pbank(bkS, slot2 * 128, 128),
                TRIUf[:, :]), r=[PBK[bkS], "TRIUf"], w=["sAB"])
            cx.group("tensor", lambda e, h=h: [
                e.matmul(pbank(3, 128 + h, 1), sAB[:, (2 * h + 1) * 128:(2 * h + 2) * 128],
                         onesColB[:, 0:1], start=False, stop=True, skip_group_check=True),
                e.matmul(pbank(3, 144 + h, 1), sAB[:, (2 * h) * 128:(2 * h + 1) * 128],
                         onesColB[:, 0:1], start=False, stop=True, skip_group_check=True)],
                r=["sAB", "onesColB"], w=[PBK[3]])

        if c == 0 and "sab0" in taps:
            cx.op("vector", lambda e: e.tensor_copy(sR[:, 0:256], sAB[:, 0:256]),
                  r=["sAB"], w=["sR"])
            cx.op("sync", lambda e: e.dma_start(out=taps["sab0"], in_=sR[:, 0:256]),
                  r=["sR"], w=["tapsab"], dma=True)
        if c == 0 and "md0" in taps:
            cx.op("vector", lambda e: e.tensor_copy(sHt[:, 0:160], pbank(3, 96, 160)),
                  r=[PBK[3]], w=["sHt"])
            cx.op("sync", lambda e: e.dma_start(out=taps["md0"], in_=sHt[:, 0:160]),
                  r=["sHt"], w=["tapmd"], dma=True)
        sl = lambda q: stq(sST, q)
        cx.op("vector", lambda e: e.tensor_add(sl(Q_D1), pbank(3, 128, 16), sl(Q_QCK)),
              r=[PBK[3], "sST"], w=["sST"])
        if c == 0 and "d1a" in taps:
            cx.op("vector", lambda e: e.tensor_copy(sTmpA[:, :], sl(Q_D1)),
                  r=["sST"], w=["sTmpA"])
            cx.op("sync", lambda e: e.dma_start(out=taps["d1a"], in_=sTmpA[:, :]),
                  r=["sTmpA"], w=["tapd1a"], dma=True)
        cx.op("vector", lambda e: e.scalar_tensor_tensor(
            sl(Q_D1), pbank(3, 96, 32)[:, 0::2], EPS, sl(Q_D1), op0=ALU.mult, op1=ALU.add),
            r=[PBK[3], "sST"], w=["sST"])
        if c == 0 and "d1b" in taps:
            cx.op("vector", lambda e: e.tensor_copy(sTmpA[:, :], sl(Q_D1)),
                  r=["sST"], w=["sTmpA"])
            cx.op("sync", lambda e: e.dma_start(out=taps["d1b"], in_=sTmpA[:, :]),
                  r=["sTmpA"], w=["tapd1b"], dma=True)
        cx.op("vector", lambda e: e.tensor_add(sl(Q_D2), pbank(3, 144, 16), sl(Q_KCQ)),
              r=[PBK[3], "sST"], w=["sST"])
        cx.op("vector", lambda e: e.scalar_tensor_tensor(
            sl(Q_D2), pbank(3, 96, 32)[:, 1::2], EPS, sl(Q_D2), op0=ALU.mult, op1=ALU.add),
            r=[PBK[3], "sST"], w=["sST"])
        gcol = sG[:, c:c + 1]
        cx.op("vector", lambda e: e.tensor_copy(sTmpA[:, :], sl(Q_D1)), r=["sST"], w=["sTmpA"])
        cx.op("vector", lambda e: e.reciprocal(sTmpB[:, :], sTmpA[:, :]), r=["sTmpA"], w=["sTmpB"])
        cx.op("vector", lambda e: e.tensor_scalar(sl(Q_SINK), sTmpB[:, :], gcol, None,
                                                  op0=ALU.mult), r=["sTmpB", "sG"], w=["sST"])
        cx.op("vector", lambda e: e.tensor_copy(sTmpA[:, :], sl(Q_D2)), r=["sST"], w=["sTmpA"])
        cx.op("vector", lambda e: e.reciprocal(sTmpB[:, :], sTmpA[:, :]), r=["sTmpA"], w=["sTmpB"])
        cx.op("vector", lambda e: e.tensor_scalar(sl(Q_SRC), sTmpB[:, :], gcol, None,
                                                  op0=ALU.mult), r=["sTmpB", "sG"], w=["sST"])
        cx.op("vector", lambda e: e.tensor_mul(sl(Q_W3), sl(Q_SDK), sl(Q_SRC)),
              r=["sST"], w=["sST"])
        cx.op("vector", lambda e: e.tensor_mul(sl(Q_W4), sl(Q_SDQ), sl(Q_SINK)),
              r=["sST"], w=["sST"])
        tri_rhs2 = sST[:, :].rearrange("p (h q) -> p h q", q=NQ)[:, :, Q_W3:Q_W4 + 1]
        cx.group("tensor", lambda e: [
            e.matmul(pbank(3, 160, 32), TRIUf[:, :], tri_rhs2, start=False, stop=True, skip_group_check=True),
            e.matmul(pbank(3, 160, 32), oneRow[:1, 0:128],
                     sCR3[:, :], start=False, stop=True, skip_group_check=True)],
            r=["TRIUf", "sST", "sCR3", "oneRow"], w=[PBK[3]])
        cx.op("vector", lambda e: e.tensor_copy(sSinkB[:, :], sl(Q_SINK)),
              r=["sST"], w=["sSinkB"])
        for h in range(H):
            cx.group("tensor", lambda e, h=h: [
                e.matmul(pbank(3, 192 + h, 1), sAB[:, (2 * h) * 128:(2 * h + 1) * 128],
                         sSinkB[:, h:h + 1],
                         start=False, stop=True, skip_group_check=True)],
                r=["sAB", "sSinkB"], w=[PBK[3]])
        cx.op("vector", lambda e: e.tensor_add(sl(Q_CS), pbank(3, 192, 16), sl(Q_KCQS)),
              r=[PBK[3], "sST"], w=["sST"])
        cx.op("vector", lambda e: e.scalar_tensor_tensor(
            sl(Q_CS), sl(Q_SDK), EPS, sl(Q_CS), op0=ALU.mult, op1=ALU.add),
            r=["sST"], w=["sST"])
        cx.op("vector", lambda e: e.scalar_tensor_tensor(
            sl(Q_CS), pbank(3, 160, 32)[:, 1::2], EPS, sl(Q_CS), op0=ALU.mult, op1=ALU.add),
            r=[PBK[3], "sST"], w=["sST"])
        rgcol = sRG[:, c:c + 1]
        cx.op("vector", lambda e: e.tensor_scalar(sl(Q_CS), sl(Q_CS), rgcol, None,
                                                  op0=ALU.mult), r=["sST", "sRG"], w=["sST"])
        cx.op("vector", lambda e: e.tensor_scalar(sl(Q_CS), sl(Q_CS), 1.0, -1.0,
                                                  op0=ALU.min, op1=ALU.max),
              r=["sST"], w=["sST"])
        cx.op("scalar", lambda e: e.activation(sl(Q_E), sl(Q_CS), AF.Exp),
              r=["sST"], w=["sST"])
        cx.group("tensor", lambda e: [
            e.matmul(pbank(3, 224, 16), TRIUf[:, :],
                     sST[:, :].rearrange("p (h q) -> p h q", q=NQ)[:, :, Q_E:Q_E + 1],
                     start=False, stop=True, skip_group_check=True),
            e.matmul(pbank(3, 224, 16), oneRow[:1, 0:128],
                     sCR4[:, :], start=False, stop=True, skip_group_check=True)],
            r=["TRIUf", "sST", "sCR4", "oneRow"], w=[PBK[3]])
        cx.op("vector", lambda e: e.reciprocal(sTmpB[:, :], pbank(3, 224, 16)),
              r=[PBK[3]], w=["sTmpB"])
        cx.op("vector", lambda e: e.tensor_mul(sl(Q_T1), sTmpB[:, :], sl(Q_E)),
              r=["sTmpB", "sST"], w=["sST"])
        cx.op("vector", lambda e: e.tensor_scalar(sl(Q_T1), sl(Q_T1), gcol, None,
                                                  op0=ALU.mult), r=["sST", "sG"], w=["sST"])
        for h in range(H):
            cx.op("vector", lambda e, h=h: e.tensor_scalar(
                sVs[:, h * 65: h * 65 + 64], sV[:, h * 64:(h + 1) * 64],
                sST[:, h * NQ + Q_T1: h * NQ + Q_T1 + 1], None, op0=ALU.mult),
                r=["sV", "sST"], w=["sVs"])
        cx.op("vector", lambda e: e.tensor_copy(
            sVs[:, :].rearrange("p (h n) -> p h n", n=65)[:, :, 64], sl(Q_SRC)),
            r=["sST"], w=["sVs"])
        cx.op("vector", lambda e: e.tensor_copy(
            sOS[:, :].rearrange("p (h n) -> p h n", n=2)[:, :, 1], sl(Q_SINK)),
            r=["sST"], w=["sOS"])
        cx.op("vector", lambda e: e.tensor_copy(
            sOS2[:, :].rearrange("p (h n) -> p h n", n=2)[:, :, 1], sl(Q_SRC)),
            r=["sST"], w=["sOS2"])

        for h in range(H):
            par, g = h % 2, h // 2
            rows = slice(par * 64, par * 64 + 64)
            slot = h % 4
            okey = PBK[4]
            cx.group("tensor", lambda e, h=h, slot=slot, rows=rows, g=g: [
                e.matmul(pbank(4, slot * 65, 65),
                         sAB[:, (2 * h + 1) * 128:(2 * h + 2) * 128],
                         sVs[:, h * 65:(h + 1) * 65], start=True, stop=False),
                e.matmul(pbank(4, slot * 65, 64), sQT[rows, g * 128:(g + 1) * 128],
                         sKV2[rows, (g) * DH:(g + 1) * DH], start=False, stop=True)],
                r=["sAB", "sVs", "sQT", "sKV2"], w=[okey])
            cx.op("scalar", lambda e, h=h, slot=slot: e.copy(
                sOutRaw[:, h * 65:(h + 1) * 65], pbank(4, slot * 65, 65)),
                r=[okey], w=["sOutRaw"])

        u1 = sOutRaw[:, :].rearrange("p (h n) -> p h n", n=65)[:, :, 64]
        cx.op("vector", lambda e: e.tensor_add(sTmpA[:, :], u1, sl(Q_QCKS)),
              r=["sOutRaw", "sST"], w=["sTmpA"])
        cx.op("vector", lambda e: e.scalar_tensor_tensor(
            sTmpA[:, :], sl(Q_SDQ), EPS, sTmpA[:, :], op0=ALU.mult, op1=ALU.add),
            r=["sST", "sTmpA"], w=["sTmpA"])
        cx.op("vector", lambda e: e.scalar_tensor_tensor(
            sTmpA[:, :], pbank(3, 160, 32)[:, 0::2], EPS, sTmpA[:, :],
            op0=ALU.mult, op1=ALU.add), r=[PBK[3], "sTmpA"], w=["sTmpA"])
        cx.op("vector", lambda e: e.tensor_scalar(sTmpA[:, :], sTmpA[:, :], rgcol, None,
                                                  op0=ALU.mult), r=["sTmpA", "sRG"], w=["sTmpA"])
        cx.op("scalar", lambda e: e.activation(sTmpB[:, :], sTmpA[:, :], AF.Tanh, scale=0.5),
              r=["sTmpA"], w=["sTmpB"])
        cx.op("vector", lambda e: e.tensor_scalar(sTmpB[:, :], sTmpB[:, :], 0.5, 0.5,
                                                  op0=ALU.mult, op1=ALU.add),
              r=["sTmpB"], w=["sTmpB"])
        cx.op("vector", lambda e: e.tensor_mul(sGam[:, :], sTmpB[:, :], sl(Q_SINK)),
              r=["sTmpB", "sST"], w=["sGam"])
        cx.op("vector", lambda e: e.tensor_scalar(sGam[:, :], sGam[:, :], rgcol, None,
                                                  op0=ALU.mult), r=["sGam", "sRG"], w=["sGam"])
        for h in range(H):
            cx.op("vector", lambda e, h=h: e.tensor_scalar(
                sXattn[:, h * 64:(h + 1) * 64], sOutRaw[:, h * 65: h * 65 + 64],
                sGam[:, h:h + 1], None, op0=ALU.mult),
                r=["sOutRaw", "sGam"], w=["sXattn"])

        for h in range(H):
            bk_i = 6 if h < 8 else 7
            cx.group("tensor", lambda e, h=h, bk_i=bk_i: [
                e.matmul(pbank(bk_i, (h % 8) * DH, DH)[0:64, :], sKtok[:, h * DH:(h + 1) * DH],
                         sVs[:, h * 65: h * 65 + 64], start=False, stop=True, skip_group_check=True)],
                r=["sKtok", "sVs", "sKV2"], w=[PBK[bk_i]])
        for h in range(H):
            cx.group("tensor", lambda e, h=h: [
                e.matmul(pbank(3, 240 + h * 4, 2)[0:64, :], sKtok[:, h * DH:(h + 1) * DH],
                         sOS2[:, h * 2: h * 2 + 2], start=False, stop=True, skip_group_check=True),
                e.matmul(pbank(3, 240 + h * 4 + 2, 2)[0:64, :], sQtok[:, h * DH:(h + 1) * DH],
                         sOS[:, h * 2: h * 2 + 2], start=False, stop=True, skip_group_check=True)],
                r=["sKtok", "sQtok", "sOS", "sOS2"], w=[PBK[3]])
        rrow = sST[:, :].rearrange("p (h q) -> p h q", q=NQ)[:, :, 0:5]
        cx.group("tensor", lambda e: [
            e.matmul(pbank(3, 432, 80)[0:1, :], onesCol[:, 0:1], rrow,
                     start=False, stop=True, skip_group_check=True)],
            r=["sST", "onesCol"], w=[PBK[3]])
        cx.op("vector", lambda e: e.tensor_add(sCarry[:, :], sCarry[:, :],
                                               pbank(3, 240, 64)[0:64, :]),
              r=[PBK[3], "sCarry"], w=["sCarry"])
        cx.op("vector", lambda e: e.tensor_add(sCRow[:, :], sCRow[:, :],
                                               pbank(3, 432, 80)[0:1, :]),
              r=[PBK[3], "sCRow"], w=["sCRow"])

        if f"la{c}" in taps:
            cx.op("sync", lambda e, c=c: e.dma_start(out=taps[f"la{c}"], in_=sST[:, :]),
                  r=["sST"], w=["tapla"], dma=True)
        if c == 0 and "oraw0" in taps:
            cx.op("sync", lambda e: e.dma_start(out=taps["oraw0"], in_=sOutRaw[:, :]),
                  r=["sOutRaw"], w=["tapor"], dma=True)
        if c == 1 and "kv1" in taps:
            cx.op("vector", lambda e: e.tensor_copy(sR[:, 0:8 * DH], sKV2[:, :]),
                  r=["sKV2"], w=["sR"])
            cx.op("sync", lambda e: e.dma_start(out=taps["kv1"], in_=sR[:, 0:8 * DH]),
                  r=["sR"], w=["tapkv"], dma=True)
        if c == 1 and "crow1" in taps:
            cx.op("sync", lambda e: e.dma_start(out=taps["crow1"], in_=sCRow[:, :]),
                  r=["sCRow"], w=["tapcr"], dma=True)
        if c == 1 and "carry1" in taps:
            cx.op("sync", lambda e: e.dma_start(out=taps["carry1"], in_=sCarry[:, :]),
                  r=["sCarry"], w=["tapca"], dma=True)
        if "xattn" in taps:
            cx.op("vector", lambda e: e.tensor_copy(sR[:, :], sXattn[:, :]),
                  r=["sXattn"], w=["sR"])
            cx.op("sync", lambda e: e.dma_start(
                out=taps["xattn"][c * C:(c + 1) * C, :], in_=sR[:, :]),
                r=["sR"], w=["tapxa"], dma=True)

    def emit_wo_ln1(c):
        t0 = c * C
        xk = f"sX{c % 2}"
        for g in range(8):
            bk = rot[0]; rot[0] = 1 - rot[0]
            cx.group("tensor", lambda e, g=g, bk=bk: [
                e.transpose(pbankb(bk, 0, 128), sXattn[:, g * 128:(g + 1) * 128], idb[:, :])],
                r=["sXattn", "idb"], w=[PBK[bk]])
            cx.op("scalar", lambda e, g=g, bk=bk: e.copy(
                sXaT[:, g * 128:(g + 1) * 128], pbankb(bk, 0, 128)),
                r=[PBK[bk]], w=["sXaT"])
        for half, bk in ((0, 4), (1, 5)):
            def emits(e, half=half, bk=bk):
                insts = []
                for kb in range(8):
                    insts.append(e.matmul(
                        pbank(bk, 0, 512), sXaT[:, kb * 128:(kb + 1) * 128],
                        sWo[:, kb * D + half * 512: kb * D + half * 512 + 512],
                        start=(kb == 0), stop=False))
                insts.append(e.matmul(
                    pbank(bk, 0, 512), oneRow[:1, 0:128],
                    sBo[:, half * 512: half * 512 + 512], start=False, stop=True))
                return insts
            cx.group("tensor", emits, r=["sXaT", "sWo", "bo_r", "oneRow"], w=[PBK[bk]])
            cx.op("vector", lambda e, half=half, bk=bk: e.tensor_add(
                sR[:, half * 512: half * 512 + 512], pbank(bk, 0, 512),
                sX[c % 2][:, half * 512: half * 512 + 512]),
                r=[PBK[bk], xk], w=["sR"])
        emit_ln(sR, "sR", sHt, "sHt", sLg1, "sLg1", sLb1, "sLb1")
        cx.op("sync", lambda e: e.dma_start(out=h_sp[t0:t0 + C, :], in_=sHt[:, :]),
              r=["sHt"], w=["h_sp"], dma=True)
        if "h" in taps:
            cx.op("sync", lambda e: e.dma_start(out=taps["h"][t0:t0 + C, :], in_=sHt[:, :]),
                  r=["sHt"], w=["taph"], dma=True)

    def emit_ln(src, skey, dst, dkey, gt, gkey, bt, bkey, plain=False):
        cx.op("vector", lambda e: e.reduce_sum(sCol[:, 0:1], src[:, :], axis=AX.X),
              r=[skey], w=["sCol"])
        cx.op("vector", lambda e: e.tensor_scalar(sCol[:, 1:2], sCol[:, 0:1],
                                                  1.0 / D, None, op0=ALU.mult),
              r=["sCol"], w=["sCol"])
        cx.op("vector", lambda e: e.tensor_scalar(dst[:, :], src[:, :], sCol[:, 1:2],
                                                  None, op0=ALU.subtract),
              r=[skey, "sCol"], w=[dkey])
        cx.op("vector", lambda e: e.scalar_tensor_tensor(
            sSq[:, :], dst[:, :], 1.0, dst[:, :], op0=ALU.mult, op1=ALU.mult,
            accum_out=sCol[:, 2:3]), r=[dkey], w=["sSq", "sCol"])
        cx.op("vector", lambda e: e.tensor_scalar(sCol[:, 2:3], sCol[:, 2:3],
                                                  1.0 / D, LNEPS,
                                                  op0=ALU.mult, op1=ALU.add),
              r=["sCol"], w=["sCol"])
        cx.op("scalar", lambda e: e.activation(sCol[:, 3:4], sCol[:, 2:3], AF.Ln),
              r=["sCol"], w=["sCol"])
        cx.op("scalar", lambda e: e.activation(sCol[:, 3:4], sCol[:, 3:4], AF.Exp,
                                               scale=-0.5),
              r=["sCol"], w=["sCol"])
        cx.op("vector", lambda e: e.tensor_scalar(dst[:, :], dst[:, :], sCol[:, 3:4],
                                                  None, op0=ALU.mult),
              r=[dkey, "sCol"], w=[dkey])
        if not plain:
            cx.op("vector", lambda e: e.tensor_mul(dst[:, :], dst[:, :], gt[:, :]),
                  r=[dkey, gkey], w=[dkey])
            cx.op("vector", lambda e: e.tensor_add(dst[:, :], dst[:, :], bt[:, :]),
                  r=[dkey, bkey], w=[dkey])

    def early_exit():
        for c in range(NCH):
            cx.op("sync", lambda e, c=c: e.dma_start(
                out=sFin[:, :], in_=X_in[c * C:(c + 1) * C, :]),
                r=["OUT"], w=["sFin"], dma=True)
            cx.op("sync", lambda e, c=c: e.dma_start(
                out=OUT[c * C:(c + 1) * C, :], in_=sFin[:, :]),
                r=["sFin"], w=["OUT"], dma=True)
        cx.barrier()
        cx.finalize(block, sems)
        return nc

    for c in range(NCH):
        emit_p1_chunk(c)
    cx.barrier()
    if PHASE_LIMIT == "p1":
        return early_exit()
    es_p1.close()
    es_p2a = ExitStack()
    stacks["cur"] = es_p2a

    sW1 = load_folded("sW1", W["W1"], D, FF)
    sW2 = load_folded("sW2", W["W2"], FF, D)
    sB1c = sb("sB1c", [128, 32])
    for fb in range(32):
        cx.op("sync", lambda e, fb=fb: e.dma_start(
            out=sB1c[:, fb:fb + 1],
            in_=W["b1"][0:1, fb * 128:(fb + 1) * 128].rearrange("o p -> p o")),
            w=["sB1c"], dma=True)
    sB2r = sb("sB2r", [1, D])
    cx.op("sync", lambda e: e.dma_start(out=sB2r[:, :], in_=W["b2"]), w=["b2_r"], dma=True)
    sMidT = sb("sMidT", [128, 32 * 128], BF16)

    def emit_p2a_tile(c):
        t0 = c * C
        pX = sX[c % 2]
        xk = f"sX{c % 2}"
        cx.op("sync", lambda e: e.dma_start(out=pX[:, :], in_=h_sp[t0:t0 + C, :]),
              r=["h_sp"], w=[xk], dma=True)
        cx.op("vector", lambda e: e.tensor_copy(sHb[:, :], pX[:, :]), r=[xk], w=["sHb"])
        for g in range(8):
            bk = rot[0]; rot[0] = 1 - rot[0]
            cx.group("tensor", lambda e, g=g, bk=bk: [
                e.transpose(pbankb(bk, 0, 128), sHb[:, g * 128:(g + 1) * 128], idb[:, :])],
                r=["sHb", "idb"], w=[PBK[bk]])
            cx.op("scalar", lambda e, g=g, bk=bk: e.copy(
                sXT[:, g * 128:(g + 1) * 128], pbankb(bk, 0, 128)),
                r=[PBK[bk]], w=["sXT"])
        for fb in range(32):
            def evac(bk, fb=fb):
                cx.op("scalar", lambda e: e.activation(
                    sMidT[:, fb * 128:(fb + 1) * 128], pbank(bk, 0, 128), AF.Gelu,
                    bias=sB1c[:, fb:fb + 1]), r=[PBK[bk], "sB1c"], w=["sMidT"])
            mm128(lambda kb, fb=fb: sW1[:, kb * FF + fb * 128: kb * FF + fb * 128 + 128],
                  lambda kb: sXT[:, kb * 128:(kb + 1) * 128],
                  ["sW1", "sXT"], evac)
        for half, bk in ((0, 4), (1, 5)):
            def emits(e, half=half, bk=bk):
                insts = []
                for fb in range(32):
                    insts.append(e.matmul(
                        pbank(bk, 0, 512), sMidT[:, fb * 128:(fb + 1) * 128],
                        sW2[:, fb * D + half * 512: fb * D + half * 512 + 512],
                        start=(fb == 0), stop=False))
                insts.append(e.matmul(
                    pbank(bk, 0, 512), oneRow[:1, 0:128],
                    sB2r[:, half * 512: half * 512 + 512], start=False, stop=True))
                return insts
            cx.group("tensor", emits, r=["sMidT", "sW2", "b2_r", "oneRow"], w=[PBK[bk]])
            cx.op("vector", lambda e, half=half, bk=bk: e.tensor_add(
                sR[:, half * 512: half * 512 + 512], pbank(bk, 0, 512),
                pX[:, half * 512: half * 512 + 512]),
                r=[PBK[bk], xk], w=["sR"])
        cx.op("sync", lambda e: e.dma_start(out=s2_sp[t0:t0 + C, :], in_=sR[:, :]),
              r=["sR"], w=["s2_sp"], dma=True)
        if "s2" in taps:
            cx.op("sync", lambda e: e.dma_start(out=taps["s2"][t0:t0 + C, :], in_=sR[:, :]),
                  r=["sR"], w=["taps2"], dma=True)

    for c in range(NCH):
        emit_p2a_tile(c)
    cx.barrier()
    if PHASE_LIMIT == "p2a":
        return early_exit()
    es_p2a.close()
    es_v65 = ExitStack()
    stacks["cur"] = es_v65
    sV65 = sb("sV65", [128, NCH * H * 65], BF16)
    es_p2b = ExitStack()
    stacks["cur"] = es_p2b

    sWqkv = load_folded("sWqkv", W["Wqkv"], D, 3 * D)
    sLg2 = sb("sLg2", [128, D]); sLb2 = sb("sLb2", [128, D])
    sFg1 = sb("sFg1", [128, D]); sFb1 = sb("sFb1", [128, D])
    sRow2 = sb("sRow2", [1, D])
    for nm, dst, dk in (("lg2", sLg2, "sLg2"), ("lb2", sLb2, "sLb2"),
                        ("fg1", sFg1, "sFg1"), ("fb1", sFb1, "sFb1")):
        cx.op("sync", lambda e, nm=nm: e.dma_start(out=sRow2[:, :], in_=W[nm]),
              r=["sRow2"], w=["sRow2"], dma=True)
        replicate_row(sRow2, dst, "sRow2", dk, bank=0)
    sSin = sb("sSin", [128, NCH * 64])
    sCos = sb("sCos", [128, NCH * 64])
    sAng = sb("sAng", [128, 64])
    sKk = sb("sKk", [128, 32])
    for c in range(NCH):
        pass
    sInv = sb("sInv", [1, 32])
    cx.op("gpsimd", lambda e: e.iota(sInv[:, :], pattern=[[1, 32]], base=0,
                                     channel_multiplier=0,
                                     allow_small_or_imprecise_dtypes=True), w=["sInv"])
    LN1e4 = float(np.log(10000.0))
    cx.op("scalar", lambda e: e.activation(sInv[:, :], sInv[:, :], AF.Exp,
                                           scale=-LN1e4 / 32.0), r=["sInv"], w=["sInv"])
    for c in range(NCH):
        pass
    sTokR = sb("sTokR", [1, 128])
    for c in range(NCH):
        cx.op("gpsimd", lambda e, c=c: e.iota(sTokR[:, :], pattern=[[1, 128]],
                                              base=c * 128, channel_multiplier=0,
                                              allow_small_or_imprecise_dtypes=True),
              w=["sTokR"])
        cx.group("tensor", lambda e: [
            e.matmul(pbank(2, 0, 32), sTokR[:1, :], sInv[:, :], start=True, stop=True)],
            r=["sTokR", "sInv"], w=[PBK[2]])
        TWO_PI = 6.283185307179586
        MAGIC = 12582912.0
        cx.op("vector", lambda e: e.tensor_scalar(sAng[:, 0:32], pbank(2, 0, 32),
                                                  1.0 / TWO_PI, MAGIC,
                                                  op0=ALU.mult, op1=ALU.add),
              r=[PBK[2]], w=["sAng"])
        cx.op("vector", lambda e: e.tensor_scalar(sAng[:, 0:32], sAng[:, 0:32],
                                                  MAGIC, None, op0=ALU.subtract),
              r=["sAng"], w=["sAng"])
        cx.op("vector", lambda e: e.scalar_tensor_tensor(
            sAng[:, 32:64], sAng[:, 0:32], -6.283185005187988, pbank(2, 0, 32),
            op0=ALU.mult, op1=ALU.add), r=[PBK[2], "sAng"], w=["sAng"])
        cx.op("vector", lambda e: e.scalar_tensor_tensor(
            sAng[:, 32:64], sAng[:, 0:32], -3.019916e-07, sAng[:, 32:64],
            op0=ALU.mult, op1=ALU.add), r=["sAng"], w=["sAng"])
        cx.op("vector", lambda e: e.scalar_tensor_tensor(
            sAng[:, 32:64], sAng[:, 0:32], -3.7120633e-14, sAng[:, 32:64],
            op0=ALU.mult, op1=ALU.add), r=["sAng"], w=["sAng"])
        cx.op("vector", lambda e: e.tensor_scalar(
            sAng[:, 32:64], sAng[:, 32:64], 3.1415925, -3.1415925,
            op0=ALU.min, op1=ALU.max), r=["sAng"], w=["sAng"])
        cx.op("scalar", lambda e, c=c: e.activation(
            sSin[:, c * 64 + 32: c * 64 + 64], sAng[:, 32:64], AF.Sin),
            r=["sAng"], w=["sSin"])
        cx.op("vector", lambda e, c=c: e.tensor_scalar(
            sSin[:, c * 64: c * 64 + 32], sSin[:, c * 64 + 32: c * 64 + 64],
            -1.0, None, op0=ALU.mult), r=["sSin"], w=["sSin"])
        cx.op("vector", lambda e: e.tensor_scalar(
            sAng[:, 0:32], sAng[:, 32:64], 1.5707963267948966, None, op0=ALU.add),
            r=["sAng"], w=["sAng"])
        cx.op("vector", lambda e: e.tensor_single_scalar(
            sKk[:, 0:32], sAng[:, 0:32], 3.141592653589793, op=ALU.is_gt),
            r=["sAng"], w=["sKk"])
        cx.op("vector", lambda e: e.scalar_tensor_tensor(
            sAng[:, 0:32], sKk[:, 0:32], -6.283185307179586, sAng[:, 0:32],
            op0=ALU.mult, op1=ALU.add), r=["sAng", "sKk"], w=["sAng"])
        cx.op("vector", lambda e: e.tensor_scalar(
            sAng[:, 0:32], sAng[:, 0:32], 3.1415925, -3.1415925,
            op0=ALU.min, op1=ALU.max), r=["sAng"], w=["sAng"])
        cx.op("scalar", lambda e, c=c: e.activation(
            sCos[:, c * 64: c * 64 + 32], sAng[:, 0:32], AF.Sin),
            r=["sAng"], w=["sCos"])
        cx.op("vector", lambda e, c=c: e.tensor_copy(
            sCos[:, c * 64 + 32: c * 64 + 64], sCos[:, c * 64: c * 64 + 32]),
            r=["sCos"], w=["sCos"])

    if PHASE_LIMIT == "p2btab":
        cx.barrier()
        return early_exit()
    sQKf = sb("sQKf", [128, 2 * D])
    sQKb = sb("sQKb", [128, 2 * D], BF16)
    sTT = sb("sTT", [128, D], BF16)

    def emit_p2b_tile(c):
        t0 = c * C
        UP = P2B_UPTO
        pX = sX[c % 2]
        xk = f"sX{c % 2}"
        cx.op("sync", lambda e: e.dma_start(out=pX[:, :], in_=s2_sp[t0:t0 + C, :]),
              r=["s2_sp"], w=[xk], dma=True)
        emit_ln(pX, xk, sR, "sR", sLg2, "sLg2", sLb2, "sLb2")
        cx.op("sync", lambda e: e.dma_start(out=pX[:, :], in_=X_in[t0:t0 + C, :]),
              r=["sR", xk], w=[xk], dma=True)
        cx.op("vector", lambda e: e.tensor_add(sR[:, :], sR[:, :], pX[:, :]),
              r=["sR", xk], w=["sR"])
        emit_ln(sR, "sR", sR, "sR", None, None, None, None, plain=True)
        cx.op("sync", lambda e: e.dma_start(out=x1_sp[t0:t0 + C, :], in_=sR[:, :]),
              r=["sR"], w=["x1_sp"], dma=True)
        if "x1" in taps:
            cx.op("sync", lambda e: e.dma_start(out=taps["x1"][t0:t0 + C, :], in_=sR[:, :]),
                  r=["sR"], w=["tapx1"], dma=True)
        if UP < 2:
            return
        emit_ln(sR, "sR", sHt, "sHt", sFg1, "sFg1", sFb1, "sFb1")
        cx.op("vector", lambda e: e.tensor_copy(sHb[:, :], sHt[:, :]), r=["sHt"], w=["sHb"])
        for g in range(8):
            bk = rot[0]; rot[0] = 1 - rot[0]
            cx.group("tensor", lambda e, g=g, bk=bk: [
                e.transpose(pbankb(bk, 0, 128), sHb[:, g * 128:(g + 1) * 128], idb[:, :])],
                r=["sHb", "idb"], w=[PBK[bk]])
            cx.op("scalar", lambda e, g=g, bk=bk: e.copy(
                sXT[:, g * 128:(g + 1) * 128], pbankb(bk, 0, 128)),
                r=[PBK[bk]], w=["sXT"])
        if UP < 3:
            return
        for half in range(6):
            bk = rot[0]; rot[0] = 1 - rot[0]
            def emits(e, half=half, bk=bk):
                insts = []
                for kb in range(8):
                    insts.append(e.matmul(
                        pbank(bk, 0, 512), sXT[:, kb * 128:(kb + 1) * 128],
                        sWqkv[:, kb * 3 * D + half * 512: kb * 3 * D + half * 512 + 512],
                        start=(kb == 0), stop=(kb == 7)))
                return insts
            cx.group("tensor", emits, r=["sXT", "sWqkv"], w=[PBK[bk]])
            base = half * 512
            h0 = base // 192
            h1 = (base + 511) // 192
            for h in range(h0, h1 + 1):
                for part, dst_off, v in (("q", 0, 0), ("k", 64, 1), ("v", 128, 2)):
                    lo = max(base, h * 192 + dst_off)
                    hi = min(base + 512, h * 192 + dst_off + 64)
                    if lo >= hi:
                        continue
                    j0 = lo - (h * 192 + dst_off)
                    n = hi - lo
                    if part == "q":
                        dst = sQKf[:, h * 64 + j0: h * 64 + j0 + n]
                        dk2 = "sQKf"
                        cx.op("vector", lambda e, lo=lo, n=n, dst=dst, bk=bk, base=base:
                              e.tensor_copy(dst, pbank(bk, lo - base, n)),
                              r=[PBK[bk]], w=[dk2])
                    elif part == "k":
                        dst = sQKf[:, D + h * 64 + j0: D + h * 64 + j0 + n]
                        cx.op("vector", lambda e, lo=lo, n=n, dst=dst, bk=bk, base=base:
                              e.tensor_copy(dst, pbank(bk, lo - base, n)),
                              r=[PBK[bk]], w=["sQKf"])
                    else:
                        dst = sV65[:, c * H * 65 + h * 65 + j0: c * H * 65 + h * 65 + j0 + n]
                        cx.op("vector", lambda e, lo=lo, n=n, dst=dst, bk=bk, base=base:
                              e.tensor_copy(dst, pbank(bk, lo - base, n)),
                              r=[PBK[bk]], w=["sV65"])
        if UP < 4:
            return
        cx.op("vector", lambda e, c=c: e.memset(
            sV65[:, c * H * 65: (c + 1) * H * 65].rearrange(
                "p (h n) -> p h n", n=65)[:, :, 64], 1.0), r=["sV65"], w=["sV65"])
        if UP < 5:
            return
        cs = sCos[:, c * 64:(c + 1) * 64].unsqueeze(1)
        sn = sSin[:, c * 64:(c + 1) * 64].unsqueeze(1)
        for off in (0, D):
            src3 = sQKf[:, off:off + D].rearrange("p (h n) -> p h n", h=H)
            swap = sQKf[:, off:off + D].rearrange("p (h n) -> p h n", h=H)
            t = sR
            cx.op("vector", lambda e, src3=src3: e.tensor_tensor(
                sR[:, :].rearrange("p (h n) -> p h n", h=H), src3,
                cs.broadcast_to([128, H, 64]), op=ALU.mult),
                r=["sQKf", "sCos"], w=["sR"])
            cx.op("vector", lambda e, src3=src3: e.scalar_tensor_tensor(
                sHt[:, :].rearrange("p (h n) -> p h n", h=H)[:, :, 0:32],
                src3[:, :, 32:64], 1.0,
                sn.broadcast_to([128, H, 64])[:, :, 0:32], op0=ALU.mult, op1=ALU.mult),
                r=["sQKf", "sSin"], w=["sHt"])
            cx.op("vector", lambda e, src3=src3: e.scalar_tensor_tensor(
                sHt[:, :].rearrange("p (h n) -> p h n", h=H)[:, :, 32:64],
                src3[:, :, 0:32], 1.0,
                sn.broadcast_to([128, H, 64])[:, :, 32:64], op0=ALU.mult, op1=ALU.mult),
                r=[u"sQKf", "sSin"], w=["sHt"])
            cx.op("vector", lambda e, off=off: e.tensor_add(
                sQKb[:, off:off + D], sR[:, :], sHt[:, :]),
                r=["sR", "sHt"], w=["sQKb"])
        if UP < 6:
            return
        for (off, sp, spk) in ((0, qt_sp, "qt_sp"), (D, kt_sp, "kt_sp")):
            for g in range(8):
                bk = rot[0]; rot[0] = 1 - rot[0]
                cx.group("tensor", lambda e, g=g, bk=bk, off=off: [
                    e.transpose(pbankb(bk, 0, 128),
                                sQKb[:, off + g * 128: off + (g + 1) * 128], idb[:, :])],
                    r=["sQKb", "idb"], w=[PBK[bk]])
                cx.op("scalar", lambda e, g=g, bk=bk: e.copy(
                    sTT[:, g * 128:(g + 1) * 128], pbankb(bk, 0, 128)),
                    r=[PBK[bk]], w=["sTT"])
            cx.op("sync", lambda e, sp=sp, c=c: e.dma_start(
                out=sp[c], in_=sTT[:, :]), r=["sTT"], w=[spk], dma=True)

    for c in range(NCH if PHASE_LIMIT != "p2b1" else 1):
        emit_p2b_tile(c)
    cx.barrier()
    if PHASE_LIMIT == "p2b1":
        return early_exit()
    if PHASE_LIMIT == "p2b":
        return early_exit()
    es_p2b.close()
    es_p3 = ExitStack()
    stacks["cur"] = es_p3

    negm_sp = nc.dram_tensor("negm_sp", [H, T], BF16).ap()
    sSel = sb("sSel", [128, 8 * 16], BF16)
    sIt2 = sb("sIt2", [128, 16])
    sPge = sb("sPge", [128, 1])
    cx.op("gpsimd", lambda e: e.iota(sIt2[:, :], pattern=[[1, 16]], base=0,
                                     channel_multiplier=0,
                                     allow_small_or_imprecise_dtypes=True), w=["sIt2"])
    cx.op("gpsimd", lambda e: e.iota(sPge[:, :], pattern=[[1, 1]], base=0,
                                     channel_multiplier=1,
                                     allow_small_or_imprecise_dtypes=True), w=["sPge"])
    cx.op("vector", lambda e: e.tensor_single_scalar(sPge[:, :], sPge[:, :], 63.5,
                                                     op=ALU.is_gt), r=["sPge"], w=["sPge"])
    cx.op("vector", lambda e: e.tensor_scalar(sIt2[:, :], sIt2[:, :], sPge[:, :1],
                                              None, op0=ALU.subtract),
          r=["sIt2", "sPge"], w=["sIt2"])
    for g in range(8):
        cx.op("vector", lambda e, g=g: e.tensor_single_scalar(
            sSel[:, g * 16:(g + 1) * 16], sIt2[:, :], float(2 * g), op=ALU.is_equal),
            r=["sIt2"], w=["sSel"])
    sKTb = sb("sKTb", [128, D], BF16)
    sSqB = sb("sSqB", [128, D], BF16)
    sKm = sb("sKm", [16, 2])
    sNg = sb("sNg", [16, 128])
    cx.op("vector", lambda e: e.memset(sKm[:, :], 0.0), w=["sKm"])
    for c in range(NCH):
        cx.op("sync", lambda e, c=c: e.dma_start(out=sKTb[:, :], in_=kt_sp[c]),
              r=["kt_sp"], w=["sKTb"], dma=True)
        cx.op("scalar", lambda e: e.activation(sSqB[:, :], sKTb[:, :], AF.Square),
              r=["sKTb"], w=["sSqB"])
        def emits(e):
            insts = []
            for g in range(8):
                insts.append(e.matmul(pbank(2, 0, 128)[0:16, :], sSel[:, g * 16:(g + 1) * 16],
                                      sSqB[:, g * 128:(g + 1) * 128],
                                      start=(g == 0), stop=(g == 7)))
            return insts
        cx.group("tensor", emits, r=["sSel", "sSqB"], w=[PBK[2]])
        cx.op("vector", lambda e: e.reduce_max(sKm[0:16, 1:2], pbank(2, 0, 128)[0:16, :],
                                               axis=AX.X), r=[PBK[2]], w=["sKm"])
        cx.op("vector", lambda e: e.tensor_max(sKm[0:16, 0:1], sKm[0:16, 0:1],
                                               sKm[0:16, 1:2]), r=["sKm"], w=["sKm"])
    cx.op("vector", lambda e: e.tensor_scalar(sKm[0:16, 1:2], sKm[0:16, 0:1], 1e-20,
                                              None, op0=ALU.add), r=["sKm"], w=["sKm"])
    cx.op("scalar", lambda e: e.activation(sKm[0:16, 1:2], sKm[0:16, 1:2], AF.Ln),
          r=["sKm"], w=["sKm"])
    cx.op("scalar", lambda e: e.activation(sKm[0:16, 1:2], sKm[0:16, 1:2], AF.Exp,
                                           scale=0.5), r=["sKm"], w=["sKm"])
    cx.op("vector", lambda e: e.tensor_scalar(sKm[0:16, 1:2], sKm[0:16, 1:2], -1.0,
                                              None, op0=ALU.mult), r=["sKm"], w=["sKm"])
    for c in range(NCH):
        cx.op("sync", lambda e, c=c: e.dma_start(out=sKTb[:, :], in_=qt_sp[c]),
              r=["kt_sp", "sSqB", "sKTb"], w=["sKTb"], dma=True)
        cx.op("scalar", lambda e: e.activation(sSqB[:, :], sKTb[:, :], AF.Square),
              r=["sKTb"], w=["sSqB"])
        def emits(e):
            insts = []
            for g in range(8):
                insts.append(e.matmul(pbank(2, 0, 128)[0:16, :], sSel[:, g * 16:(g + 1) * 16],
                                      sSqB[:, g * 128:(g + 1) * 128],
                                      start=(g == 0), stop=(g == 7)))
            return insts
        cx.group("tensor", emits, r=["sSel", "sSqB"], w=[PBK[2]])
        cx.op("vector", lambda e: e.tensor_scalar(sNg[0:16, :], pbank(2, 0, 128)[0:16, :],
                                                  1e-20, None, op0=ALU.add),
              r=[PBK[2]], w=["sNg"])
        cx.op("scalar", lambda e: e.activation(sNg[0:16, :], sNg[0:16, :], AF.Ln),
              r=["sNg"], w=["sNg"])
        cx.op("scalar", lambda e: e.activation(sNg[0:16, :], sNg[0:16, :], AF.Exp,
                                               scale=0.5), r=["sNg"], w=["sNg"])
        cx.op("vector", lambda e: e.tensor_scalar(sNg[0:16, :], sNg[0:16, :],
                                                  sKm[0:16, 1:2], None, op0=ALU.mult),
              r=["sNg", "sKm"], w=["sNg"])
        cx.op("gpsimd", lambda e, c=c: e.dma_start(out=negm_sp[:, c * C:(c + 1) * C],
                                                   in_=sNg[0:16, :]),
              r=["sNg"], w=["negm_sp"], dma=True)

    sKaug = sb("sKaug", [65, T], BF16)
    sQaug = sb("sQaug", [65, 512], BF16)
    sPT = sb("sPT", [128, 2 * 512], BF16)
    sAOb = sb("sAOb", [128, 64], BF16)

    def emit_p3_head(h):
        par, g = h % 2, h // 2
        cx.op("vector", lambda e: e.memset(sKaug[64:65, :], 1.0),
              r=["sKaug"], w=["sKaug"])
        for c in range(NCH):
            cx.op("sync", lambda e, c=c: e.dma_start(
                out=sKaug[0:64, c * C:(c + 1) * C],
                in_=kt_sp[c, par * 64:par * 64 + 64, g * 128:(g + 1) * 128]),
                r=["kt_sp", "sKaug"], w=["sKaug"], dma=True)
        for qg in range(8):
            for j in range(4):
                cx.op("sync", lambda e, qg=qg, j=j: e.dma_start(
                    out=sQaug[0:64, j * 128:(j + 1) * 128],
                    in_=qt_sp[qg * 4 + j, par * 64:par * 64 + 64, g * 128:(g + 1) * 128]),
                    r=["qt_sp", "sQaug"], w=["sQaug"], dma=True)
            cx.op("sync", lambda e, qg=qg, h=h: e.dma_start(
                out=sQaug[64:65, :], in_=negm_sp[h:h + 1, qg * 512:(qg + 1) * 512]),
                r=["negm_sp", "sQaug"], w=["sQaug"], dma=True)
            kmax_c = 4 * qg + 3
            for kc in range(kmax_c + 1):
                bk = kc % 2
                r_ = kc - 4 * qg
                cx.group("tensor", lambda e, kc=kc, bk=bk: [
                    e.matmul(pbank(bk, 0, 512), sKaug[:, kc * 128:(kc + 1) * 128],
                             sQaug[:, :], start=True, stop=True)],
                    r=["sKaug", "sQaug"], w=[PBK[bk]])
                slot = kc % 2
                if r_ >= 0:
                    cx.op("vector", lambda e, bk=bk, r_=r_: e.tensor_add(
                        pbank(bk, r_ * 128, 128), pbank(bk, r_ * 128, 128), SLOB[:, :]),
                        r=[PBK[bk], "SLOB"], w=[PBK[bk]])
                    if r_ > 0:
                        cx.op("vector", lambda e, slot=slot, r_=r_: e.memset(
                            sPT[:, slot * 512: slot * 512 + r_ * 128], 0.0),
                            r=["sPT"], w=["sPT"])
                    cx.op("scalar", lambda e, bk=bk, slot=slot, r_=r_: e.activation(
                        sPT[:, slot * 512 + r_ * 128: slot * 512 + 512],
                        pbank(bk, r_ * 128, 512 - r_ * 128), AF.Exp, scale=0.125),
                        r=[PBK[bk], "sPT"], w=["sPT"])
                else:
                    cx.op("scalar", lambda e, bk=bk, slot=slot: e.activation(
                        sPT[:, slot * 512: slot * 512 + 512], pbank(bk, 0, 512),
                        AF.Exp, scale=0.125), r=[PBK[bk], "sPT"], w=["sPT"])
                def emits(e, kc=kc, slot=slot, r_=r_, h=h):
                    insts = []
                    for j in range(max(r_, 0), 4):
                        insts.append(e.matmul(
                            pbank(2, j * 65, 65), sPT[:, slot * 512 + j * 128: slot * 512 + (j + 1) * 128],
                            sV65[:, kc * H * 65 + h * 65: kc * H * 65 + (h + 1) * 65],
                            start=(kc == 0 and j == 0), stop=(kc == kmax_c and j == 3),
                            skip_group_check=True))
                    return insts
                cx.group("tensor", emits, r=["sPT", "sV65"], w=[PBK[2]])
            for j in range(4):
                cx.op("vector", lambda e, j=j: e.reciprocal(sCol[:, 0:1],
                                                            pbank(2, j * 65 + 64, 1)),
                      r=[PBK[2]], w=["sCol"])
                cx.op("vector", lambda e, j=j: e.tensor_scalar(
                    sAOb[:, :], pbank(2, j * 65, 64), sCol[:, 0:1], None, op0=ALU.mult),
                    r=[PBK[2], "sCol"], w=["sAOb"])
                cx.op("sync", lambda e, qg=qg, j=j, h=h: e.dma_start(
                    out=ao_sp[(qg * 4 + j) * 128:(qg * 4 + j + 1) * 128,
                              h * 64:(h + 1) * 64], in_=sAOb[:, :]),
                    r=["sAOb"], w=["ao_sp"], dma=True)

    for h in range(H):
        emit_p3_head(h)
    cx.barrier()
    if PHASE_LIMIT == "p3":
        return early_exit()
    if "attno" in taps:
        for c in range(NCH):
            cx.op("sync", lambda e, c=c: e.dma_start(out=sHb[:, :],
                                                     in_=ao_sp[c * C:(c + 1) * C, :]),
                  r=["ao_sp"], w=["sHb"], dma=True)
            cx.op("vector", lambda e: e.tensor_copy(sR[:, :], sHb[:, :]),
                  r=["sHb"], w=["sR"])
            cx.op("sync", lambda e, c=c: e.dma_start(
                out=taps["attno"][c * C:(c + 1) * C, :], in_=sR[:, :]),
                r=["sR"], w=["tapao"], dma=True)
        cx.barrier()

    sWproj = load_folded("sWproj", W["Wproj"], D, D)
    sWg = load_folded("sWg", W["Wg"], D, E)
    sRow2b = sb("sRow2b", [1, D])
    sFg2 = sb("sFg2", [128, D]); sFb2 = sb("sFb2", [128, D])
    for nm, dst, dk in (("fg2", sFg2, "sFg2"), ("fb2", sFb2, "sFb2")):
        cx.op("sync", lambda e, nm=nm: e.dma_start(out=sRow2b[:, :], in_=W[nm]),
              r=["sRow2b"], w=["sRow2b"], dma=True)
        replicate_row(sRow2b, dst, "sRow2b", dk, bank=0)
    sAot = sb("sAot", [128, D], BF16)
    sI8 = sb("sI8", [128, 8])
    cx.op("gpsimd", lambda e: e.iota(sI8[:, :], pattern=[[1, 8]], base=0,
                                     channel_multiplier=0,
                                     allow_small_or_imprecise_dtypes=True), w=["sI8"])
    sGts = sb("sGts", [128, 8]); sEx = sb("sEx", [128, 8]); sPr = sb("sPr", [128, 8])
    sM1 = sb("sM1", [128, 8]); sM2 = sb("sM2", [128, 8]); sMk = sb("sMk", [128, 8])
    sT1b = sb("sT1b", [128, 8]); sT2b = sb("sT2b", [128, 8])
    sPosAll = sb("sPosAll", [128, NCH * 8])
    sROr = sb("sROr", [1, 8])
    cx.group("tensor", lambda e: [
        e.matmul(pbank(6, 0, 8)[0:1, :], oneRow[:1, 0:1], zRow[:, 0:8],
                 start=True, stop=True)], r=["oneRow", "zRow"], w=[PBK[6]])

    def emit_p3b_tile(c):
        t0 = c * C
        pX = sX[c % 2]
        xk = f"sX{c % 2}"
        cx.op("sync", lambda e: e.dma_start(out=sAot[:, :], in_=ao_sp[t0:t0 + C, :]),
              r=["ao_sp", "sAot"], w=["sAot"], dma=True)
        for g in range(8):
            bk = rot[0]; rot[0] = 1 - rot[0]
            cx.group("tensor", lambda e, g=g, bk=bk: [
                e.transpose(pbankb(bk, 0, 128), sAot[:, g * 128:(g + 1) * 128], idb[:, :])],
                r=["sAot", "idb"], w=[PBK[bk]])
            cx.op("scalar", lambda e, g=g, bk=bk: e.copy(
                sXT[:, g * 128:(g + 1) * 128], pbankb(bk, 0, 128)),
                r=[PBK[bk]], w=["sXT"])
        cx.op("sync", lambda e: e.dma_start(out=pX[:, :], in_=x1_sp[t0:t0 + C, :]),
              r=["x1_sp"], w=[xk], dma=True)
        for half, bk in ((0, 4), (1, 5)):
            def emits(e, half=half, bk=bk):
                insts = []
                for kb in range(8):
                    insts.append(e.matmul(
                        pbank(bk, 0, 512), sXT[:, kb * 128:(kb + 1) * 128],
                        sWproj[:, kb * D + half * 512: kb * D + half * 512 + 512],
                        start=(kb == 0), stop=(kb == 7)))
                return insts
            cx.group("tensor", emits, r=["sXT", "sWproj"], w=[PBK[bk]])
            cx.op("vector", lambda e, half=half, bk=bk: e.tensor_add(
                sR[:, half * 512: half * 512 + 512], pbank(bk, 0, 512),
                pX[:, half * 512: half * 512 + 512]), r=[PBK[bk], xk], w=["sR"])
            cx.op("vector", lambda e, half=half, bk=bk: e.tensor_add(
                sHt[:, half * 512: half * 512 + 512],
                sR[:, half * 512: half * 512 + 512],
                pX[:, half * 512: half * 512 + 512]), r=["sR", xk], w=["sHt"])
        cx.op("sync", lambda e: e.dma_start(out=w_sp[t0:t0 + C, :], in_=sHt[:, :]),
              r=["sHt"], w=["w_sp"], dma=True)
        if "x2" in taps:
            cx.op("sync", lambda e: e.dma_start(out=taps["x2"][t0:t0 + C, :], in_=sR[:, :]),
                  r=["sR"], w=["tapx2"], dma=True)
        emit_ln(sR, "sR", sHt, "sHt", sFg2, "sFg2", sFb2, "sFb2")
        if "h2" in taps:
            cx.op("sync", lambda e: e.dma_start(out=taps["h2"][t0:t0 + C, :], in_=sHt[:, :]),
                  r=["sHt"], w=["taph2"], dma=True)
        cx.op("vector", lambda e: e.tensor_copy(sHb[:, :], sHt[:, :]), r=["sHt"], w=["sHb"])
        for g in range(8):
            bk = rot[0]; rot[0] = 1 - rot[0]
            cx.group("tensor", lambda e, g=g, bk=bk: [
                e.transpose(pbankb(bk, 0, 128), sHb[:, g * 128:(g + 1) * 128], idb[:, :])],
                r=["sHb", "idb"], w=[PBK[bk]])
            cx.op("scalar", lambda e, g=g, bk=bk: e.copy(
                sXT[:, g * 128:(g + 1) * 128], pbankb(bk, 0, 128)),
                r=[PBK[bk]], w=["sXT"])
            cx.op("sync", lambda e, g=g, c=c: e.dma_start(
                out=h2t_sp[g, :, c * C:(c + 1) * C], in_=sXT[:, g * 128:(g + 1) * 128]),
                r=["sXT"], w=["h2t_sp"], dma=True)
        def emits(e):
            insts = []
            for kb in range(8):
                insts.append(e.matmul(pbank(3, 0, 8), sXT[:, kb * 128:(kb + 1) * 128],
                                      sWg[:, kb * E:(kb + 1) * E],
                                      start=(kb == 0), stop=(kb == 7)))
            return insts
        cx.group("tensor", emits, r=["sXT", "sWg"], w=[PBK[3]])
        cx.op("vector", lambda e: e.tensor_copy(sGts[:, :], pbank(3, 0, 8)),
              r=[PBK[3]], w=["sGts"])
        if "gates" in taps:
            cx.op("sync", lambda e: e.dma_start(out=taps["gates"][t0:t0 + C, :],
                                                in_=sGts[:, :]),
                  r=["sGts"], w=["tapg"], dma=True)
        cx.op("vector", lambda e: e.reduce_max(sCol[:, 0:1], sGts[:, :], axis=AX.X),
              r=["sGts"], w=["sCol"])
        cx.op("vector", lambda e: e.tensor_scalar(sCol[:, 1:2], sCol[:, 0:1], -1.0,
                                                  None, op0=ALU.mult), r=["sCol"], w=["sCol"])
        cx.op("scalar", lambda e: e.activation(sEx[:, :], sGts[:, :], AF.Exp,
                                               bias=sCol[:, 1:2], accum_out=sCol[:, 2:3]),
              r=["sGts", "sCol"], w=["sEx", "sCol"])
        cx.op("vector", lambda e: e.reciprocal(sCol[:, 3:4], sCol[:, 2:3]),
              r=["sCol"], w=["sCol"])
        cx.op("vector", lambda e: e.tensor_scalar(sPr[:, :], sEx[:, :], sCol[:, 3:4],
                                                  None, op0=ALU.mult),
              r=["sEx", "sCol"], w=["sPr"])
        cx.op("vector", lambda e: e.reduce_max(sCol[:, 0:1], sPr[:, :], axis=AX.X),
              r=["sPr"], w=["sCol"])
        cx.op("vector", lambda e: e.tensor_scalar(sM1[:, :], sPr[:, :], sCol[:, 0:1],
                                                  None, op0=ALU.is_ge),
              r=["sPr", "sCol"], w=["sM1"])
        cx.op("vector", lambda e: e.tensor_scalar(sT1b[:, :], sM1[:, :], -1.0, 1.0,
                                                  op0=ALU.mult, op1=ALU.add),
              r=["sM1"], w=["sT1b"])
        cx.op("vector", lambda e: e.tensor_mul(sT2b[:, :], sI8[:, :], sM1[:, :]),
              r=["sI8", "sM1"], w=["sT2b"])
        cx.op("vector", lambda e: e.scalar_tensor_tensor(sT2b[:, :], sT1b[:, :], 99.0,
                                                         sT2b[:, :], op0=ALU.mult,
                                                         op1=ALU.add),
              r=["sT1b", "sT2b"], w=["sT2b"])
        cx.op("vector", lambda e: e.tensor_reduce(sCol[:, 0:1], sT2b[:, :], axis=AX.X,
                                                  op=ALU.min), r=["sT2b"], w=["sCol"])
        cx.op("vector", lambda e: e.tensor_scalar(sM1[:, :], sI8[:, :], sCol[:, 0:1],
                                                  None, op0=ALU.is_equal),
              r=["sI8", "sCol"], w=["sM1"])
        cx.op("vector", lambda e: e.tensor_mul(sT1b[:, :], sPr[:, :], sM1[:, :]),
              r=["sPr", "sM1"], w=["sT1b"])
        cx.op("vector", lambda e: e.tensor_sub(sT1b[:, :], sPr[:, :], sT1b[:, :]),
              r=["sPr", "sT1b"], w=["sT1b"])
        cx.op("vector", lambda e: e.reduce_max(sCol[:, 0:1], sT1b[:, :], axis=AX.X),
              r=["sT1b"], w=["sCol"])
        cx.op("vector", lambda e: e.tensor_scalar(sM2[:, :], sT1b[:, :], sCol[:, 0:1],
                                                  None, op0=ALU.is_ge),
              r=["sT1b", "sCol"], w=["sM2"])
        cx.op("vector", lambda e: e.tensor_scalar(sT2b[:, :], sM2[:, :], -1.0, 1.0,
                                                  op0=ALU.mult, op1=ALU.add),
              r=["sM2"], w=["sT2b"])
        cx.op("vector", lambda e: e.tensor_mul(sEx[:, :], sI8[:, :], sM2[:, :]),
              r=["sI8", "sM2"], w=["sEx"])
        cx.op("vector", lambda e: e.scalar_tensor_tensor(sEx[:, :], sT2b[:, :], 99.0,
                                                         sEx[:, :], op0=ALU.mult,
                                                         op1=ALU.add),
              r=["sT2b", "sEx"], w=["sEx"])
        cx.op("vector", lambda e: e.tensor_reduce(sCol[:, 0:1], sEx[:, :], axis=AX.X,
                                                  op=ALU.min), r=["sEx"], w=["sCol"])
        cx.op("vector", lambda e: e.tensor_scalar(sM2[:, :], sI8[:, :], sCol[:, 0:1],
                                                  None, op0=ALU.is_equal),
              r=["sI8", "sCol"], w=["sM2"])
        cx.op("vector", lambda e: e.tensor_add(sMk[:, :], sM1[:, :], sM2[:, :]),
              r=["sM1", "sM2"], w=["sMk"])
        cx.op("vector", lambda e, c=c: e.tensor_mul(sPGall[:, c * 8:(c + 1) * 8],
                                                    sPr[:, :], sMk[:, :]),
              r=["sPr", "sMk"], w=["sPGall"])
        cx.op("vector", lambda e: e.tensor_copy(sROr[:, :], pbank(6, 0, 8)[0:1, :]),
              r=[PBK[6]], w=["sROr"])
        cx.group("tensor", lambda e: [
            e.matmul(pbank(3, 16, 8), TRIUf[:, :], sMk[:, :], start=True, stop=False),
            e.matmul(pbank(3, 16, 8), oneRow[:1, 0:128], sROr[:, :], start=False,
                     stop=True)], r=["TRIUf", "sMk", "sROr", "oneRow"], w=[PBK[3]])
        cx.op("vector", lambda e, c=c: e.tensor_copy(sPosAll[:, c * 8:(c + 1) * 8],
                                                     pbank(3, 16, 8)),
              r=[PBK[3]], w=["sPosAll"])
        cx.group("tensor", lambda e: [
            e.matmul(pbank(6, 0, 8)[0:1, :], onesCol[:, 0:1], sMk[:, :], start=False,
                     stop=True, skip_group_check=True)],
            r=["onesCol", "sMk", "sROr"], w=[PBK[6]])

    for c in range(NCH):
        emit_p3b_tile(c)
    cx.barrier()
    if PHASE_LIMIT == "p3b":
        return early_exit()

    cx.op("vector", lambda e: e.tensor_copy(sROr[:, :], pbank(6, 0, 8)[0:1, :]),
          r=[PBK[6]], w=["sROr"])
    cx.op("sync", lambda e: e.dma_start(out=cc_in, in_=sROr[:, :]),
          r=["sROr"], w=["cc_in"], dma=True)
    cx.barrier()
    def cc_emit(e):
        return e.collective_compute(
            "AllGather", ALU.bypass, replica_groups=[list(range(B))],
            ins=[cc_in.opt()], outs=[cc_out.opt()])
    cx.prog.append(("gpsimd", [("sync_d", cx.cnt["sync_d"])], cc_emit, ("cc", 1), 0))
    cx.cnt["cc"] += 1
    cx.lastw["cc_gather"] = ("cc", cx.cnt["cc"])
    sGath = sb("sGath", [8, 8])
    sSelc = sb("sSelc", [8, 1])
    sCidc = sb("sCidc", [8, 1])
    sOffs = sb("sOffs", [1, 8])
    cx.op("sync", lambda e: e.dma_start(out=sGath[:, :], in_=cc_out),
          r=["cc_gather"], w=["sGath"], dma=True)
    cx.op("sync", lambda e: e.dma_start(out=sCidc[:, :],
                                        in_=cid.rearrange("o e -> e o")),
          w=["sCidc"], dma=True)
    cx.op("gpsimd", lambda e: e.iota(sSelc[:, :], pattern=[[1, 1]], base=0,
                                     channel_multiplier=1,
                                     allow_small_or_imprecise_dtypes=True), w=["sSelc"])
    cx.op("vector", lambda e: e.tensor_scalar(sSelc[:, :], sSelc[:, :], sCidc[:, 0:1],
                                              None, op0=ALU.is_lt),
          r=["sSelc", "sCidc"], w=["sSelc"])
    cx.group("tensor", lambda e: [
        e.matmul(pbank(3, 32, 8)[0:1, :], sSelc[:, :], sGath[:, :], start=True,
                 stop=True)], r=["sSelc", "sGath"], w=[PBK[3]])
    cx.op("vector", lambda e: e.tensor_copy(sOffs[:, :], pbank(3, 32, 8)[0:1, :]),
          r=[PBK[3]], w=["sOffs"])
    for c in range(NCH):
        cx.group("tensor", lambda e: [
            e.matmul(pbank(3, 40, 8), oneRow[:1, 0:128], sOffs[:, :], start=True,
                     stop=True)], r=["oneRow", "sOffs"], w=[PBK[3]])
        cx.op("vector", lambda e, c=c: e.tensor_add(sT1b[:, :], pbank(3, 40, 8),
                                                    sPosAll[:, c * 8:(c + 1) * 8]),
              r=[PBK[3], "sPosAll"], w=["sT1b"])
        cx.op("vector", lambda e: e.tensor_single_scalar(sT2b[:, :], sT1b[:, :],
                                                         float(CAP), op=ALU.is_le),
              r=["sT1b"], w=["sT2b"])
        cx.op("vector", lambda e, c=c: e.tensor_mul(sPGall[:, c * 8:(c + 1) * 8],
                                                    sPGall[:, c * 8:(c + 1) * 8],
                                                    sT2b[:, :]),
              r=["sPGall", "sT2b"], w=["sPGall"])
        if "routing" in taps:
            cx.op("vector", lambda e, c=c: e.tensor_copy(
                sR[:, 0:8], sPGall[:, c * 8:(c + 1) * 8]), r=["sPGall"], w=["sR"])
            cx.op("vector", lambda e, c=c: e.tensor_copy(
                sR[:, 8:16], sPosAll[:, c * 8:(c + 1) * 8]), r=["sPosAll", "sR"], w=["sR"])
            cx.op("vector", lambda e, c=c: e.tensor_copy(sR[:, 16:24], sT2b[:, :]),
                  r=["sT2b", "sR"], w=["sR"])
            cx.op("sync", lambda e, c=c: e.dma_start(
                out=taps["routing"][c * C:(c + 1) * C, :], in_=sR[:, 0:24]),
                r=["sR"], w=["tapr"], dma=True)
    cx.barrier()
    es_p3.close()
    es_v65.close()
    es_p4 = ExitStack()
    stacks["cur"] = es_p4

    sWe1 = sb("sWe1", [128, 8 * FF], BF16)
    sWe2 = sb("sWe2", [128, 32 * D], BF16)
    sEB1 = sb("sEB1", [128, 32])
    sEB2r = sb("sEB2r", [1, D])
    sH2T = sb("sH2T", [128, 8 * 512], BF16)
    sMidT2 = sb("sMidT2", [128, 32 * 512], BF16)
    sMoE = sb("sMoE", [128, 512])

    def emit_p4_expert(eidx):
        cx.op("gpsimd", lambda e: e.dma_start(
            out=sWe1[:, :].rearrange("p (g n) -> p g n", g=8),
            in_=W["eW1"][eidx * D:(eidx + 1) * D, :].rearrange("(g p) n -> p g n", p=128)),
            r=["sWe1"], w=["sWe1"], dma=True)
        cx.op("gpsimd", lambda e: e.dma_start(
            out=sWe2[:, :].rearrange("p (g n) -> p g n", g=32),
            in_=W["eW2"][eidx * FF:(eidx + 1) * FF, :].rearrange("(g p) n -> p g n", p=128)),
            r=["sWe2"], w=["sWe2"], dma=True)
        for fb in range(32):
            cx.op("sync", lambda e, fb=fb: e.dma_start(
                out=sEB1[:, fb:fb + 1],
                in_=W["eb1"][eidx:eidx + 1, fb * 128:(fb + 1) * 128].rearrange("o p -> p o")),
                r=["sEB1"], w=["sEB1"], dma=True)
        cx.op("sync", lambda e: e.dma_start(out=sEB2r[:, :],
                                            in_=W["eb2"][eidx:eidx + 1, :]),
              r=["sEB2r"], w=["sEB2r"], dma=True)
        for st in range(8):
            for kb in range(8):
                cx.op("sync", lambda e, kb=kb, st=st: e.dma_start(
                    out=sH2T[:, kb * 512:(kb + 1) * 512],
                    in_=h2t_sp[kb, :, st * 512:(st + 1) * 512]),
                    r=["h2t_sp", "sH2T"], w=["sH2T"], dma=True)
            for fb in range(32):
                bk = rot[0]; rot[0] = 1 - rot[0]
                def emits(e, fb=fb, bk=bk):
                    insts = []
                    for kb in range(8):
                        insts.append(e.matmul(
                            pbank(bk, 0, 512),
                            sWe1[:, kb * FF + fb * 128: kb * FF + fb * 128 + 128],
                            sH2T[:, kb * 512:(kb + 1) * 512],
                            start=(kb == 0), stop=(kb == 7)))
                    return insts
                cx.group("tensor", emits, r=["sWe1", "sH2T"], w=[PBK[bk]])
                cx.op("scalar", lambda e, fb=fb, bk=bk: e.activation(
                    sMidT2[:, fb * 512:(fb + 1) * 512], pbank(bk, 0, 512), AF.Gelu,
                    bias=sEB1[:, fb:fb + 1]), r=[PBK[bk], "sEB1"], w=["sMidT2"])
            for m in range(4):
                gcol = (st * 4 + m) * 8 + eidx
                for half, bk in ((0, 4), (1, 5)):
                    def emits(e, m=m, half=half, bk=bk):
                        insts = []
                        for fb in range(32):
                            insts.append(e.matmul(
                                pbank(bk, 0, 512),
                                sMidT2[:, fb * 512 + m * 128: fb * 512 + (m + 1) * 128],
                                sWe2[:, fb * D + half * 512: fb * D + half * 512 + 512],
                                start=(fb == 0), stop=False))
                        insts.append(e.matmul(
                            pbank(bk, 0, 512), oneRow[:1, 0:128],
                            sEB2r[:, half * 512: half * 512 + 512], start=False,
                            stop=True))
                        return insts
                    cx.group("tensor", emits, r=["sMidT2", "sWe2", "sEB2r", "oneRow"],
                             w=[PBK[bk]])
                    cx.op("vector", lambda e, gcol=gcol, bk=bk: e.tensor_scalar(
                        sMoE[:, :], pbank(bk, 0, 512), sPGall[:, gcol:gcol + 1], None,
                        op0=ALU.mult), r=[PBK[bk], "sPGall"], w=["sMoE"])
                    tok0 = (st * 4 + m) * 128
                    if eidx == 0:
                        cx.op("gpsimd", lambda e, tok0=tok0, half=half: e.dma_start(
                            out=acc_sp[tok0:tok0 + 128, half * 512:(half + 1) * 512],
                            in_=sMoE[:, :]), r=["sMoE"], w=["acc_sp"], dma=True)
                    else:
                        cx.op("gpsimd", lambda e, tok0=tok0, half=half: e.dma_start(
                            out=acc_sp[tok0:tok0 + 128, half * 512:(half + 1) * 512],
                            in_=sMoE[:, :], accum_op=ALU.add),
                            r=["sMoE"], w=["acc_sp"], dma=True)

    for eidx in range(E):
        emit_p4_expert(eidx)
    cx.barrier()
    if PHASE_LIMIT == "p4":
        return early_exit()

    for c in range(NCH):
        t0 = c * C
        pX = sX[c % 2]
        xk = f"sX{c % 2}"
        cx.op("sync", lambda e, t0=t0: e.dma_start(out=pX[:, :], in_=w_sp[t0:t0 + C, :]),
              r=["w_sp"], w=[xk], dma=True)
        cx.op("sync", lambda e, t0=t0: e.dma_start(out=sR[:, :], in_=acc_sp[t0:t0 + C, :]),
              r=["acc_sp", "sR"], w=["sR"], dma=True)
        if "moe" in taps:
            cx.op("sync", lambda e, t0=t0: e.dma_start(
                out=taps["moe"][t0:t0 + C, :], in_=sR[:, :]), r=["sR"], w=["tapm"], dma=True)
        cx.op("vector", lambda e: e.tensor_add(sR[:, :], sR[:, :], pX[:, :]),
              r=["sR", xk], w=["sR"])
        emit_ln(sR, "sR", sFin, "sFin", None, None, None, None, plain=True)
        cx.op("sync", lambda e, t0=t0: e.dma_start(out=OUT[t0:t0 + C, :], in_=sFin[:, :]),
              r=["sFin"], w=["OUT"], dma=True)
    cx.barrier()

    cx.finalize(block, sems)
    es_p4.close()
    es.close()
    return nc


def kernel(**inputs):
    global LAST_TAPS
    nc = build_program()
    wmap = {
        "Wq": "L0_attn_Wq", "bq": "L0_attn_bq", "Wk": "L0_attn_Wk", "bk": "L0_attn_bk",
        "Wv": "L0_attn_Wv", "bv": "L0_attn_bv", "Wo": "L0_attn_Wo", "bo": "L0_attn_bo",
        "lg1": "L0_attn_ln1_g", "lb1": "L0_attn_ln1_b", "W1": "L0_attn_W1",
        "b1": "L0_attn_b1", "W2": "L0_attn_W2", "b2": "L0_attn_b2",
        "lg2": "L0_attn_ln2_g", "lb2": "L0_attn_ln2_b",
        "fg1": "L0_ffn_ln1_g", "fb1": "L0_ffn_ln1_b", "Wqkv": "L0_ffn_Wqkv",
        "Wproj": "L0_ffn_Wproj", "fg2": "L0_ffn_ln2_g", "fb2": "L0_ffn_ln2_b",
        "Wg": "L0_ffn_Wgate",
    }
    base = {}
    for k, src in wmap.items():
        a = np.ascontiguousarray(np.asarray(inputs[src], np.float32))
        if a.ndim == 1:
            a = a.reshape(1, -1)
        base[k] = a
    base["eW1"] = np.ascontiguousarray(
        np.asarray(inputs["L0_ffn_W1"], np.float32).reshape(E * D, FF))
    base["eb1"] = np.ascontiguousarray(np.asarray(inputs["L0_ffn_b1"], np.float32))
    base["eW2"] = np.ascontiguousarray(
        np.asarray(inputs["L0_ffn_W2"], np.float32).reshape(E * FF, D))
    base["eb2"] = np.ascontiguousarray(np.asarray(inputs["L0_ffn_b2"], np.float32))
    X = np.ascontiguousarray(np.asarray(inputs["L0_attn_inputs"], np.float32))
    in_maps = []
    for b in range(B):
        m = dict(base)
        m["x"] = X[b]
        m["cid"] = np.full((1, 8), float(b), np.float32)
        in_maps.append(m)
    res = run_bass_kernel_spmd(nc, in_maps, core_ids=list(range(B)))
    LAST_TAPS = {}
    for tn in DEBUG_TAPS:
        LAST_TAPS[tn] = [np.asarray(res.results[b]["tap_" + tn]) for b in range(B)]
    out = np.stack([np.asarray(res.results[b]["out"], np.float32) for b in range(B)])
    return out
